# version 15
# baseline (speedup 1.0000x reference)
import contextlib
from collections import defaultdict

import numpy as np

import jax

jax.config.update("jax_compilation_cache_dir", "/tmp/jaxcache")
jax.config.update("jax_persistent_cache_min_compile_time_secs", 2.0)

import concourse.bass as bass
import concourse.mybir as mybir
from concourse.bass_utils import run_bass_kernel_spmd

F32 = mybir.dt.float32
BF16 = mybir.dt.bfloat16
AF = mybir.ActivationFunctionType
OP = mybir.AluOpType
AX = mybir.AxisListType

B, T, D, HPC, DH, E = 2, 8192, 1024, 2, 128, 8
NT, TT, MC = 32, 512, 8
NIT = 16
NCH, CH = 8, 2048
NFT = 32
NTOK = B * T
EPS = 1e-5
ENGS = ("sync", "scalar", "vector", "tensor", "gpsimd")

ARENA = 45184


def _carve(cursor, n):
    start = cursor[0]
    cursor[0] += n
    assert cursor[0] <= ARENA, f"arena overflow: {cursor[0]}"
    return start


class Walk:

    def __init__(self, mode, eng, sems):
        self.mode = mode
        self.eng = eng
        self.sems = sems
        self.cnt = defaultdict(int)
        self.snaps = {}

    def do(self, engname, build=None, inc=None, wait=(), n=1):
        if self.mode == engname:
            for w in wait:
                s, v = w if isinstance(w, tuple) else self.snaps[w]
                if v > 0:
                    self.eng.wait_ge(self.sems[s], v)
            inst = build(self.eng) if build is not None else None
        else:
            inst = None
        if inc is not None:
            self.cnt[inc] += n
            if inst is not None:
                if isinstance(inst, list):
                    assert n == 16 * len(inst), (n, len(inst))
                    for i in inst:
                        i.then_inc(self.sems[inc], 16)
                else:
                    inst.then_inc(self.sems[inc], n)
        return inst

    def snap(self, key, sem):
        self.snaps[key] = (sem, self.cnt[sem])


def build_nc():
    nc = bass.Bass("TRN2", target_bir_lowering=False)

    xin = nc.dram_tensor("x", [B, T, D], F32, kind="ExternalInput").ap()
    xres_in = nc.dram_tensor("xres", [2048, D], F32, kind="ExternalInput").ap()
    wq_in = nc.dram_tensor("wq", [HPC, D, DH], F32, kind="ExternalInput").ap()
    wk_in = nc.dram_tensor("wk", [HPC, D, DH], F32, kind="ExternalInput").ap()
    wv_in = nc.dram_tensor("wv", [HPC, D, DH], F32, kind="ExternalInput").ap()
    wo_in = nc.dram_tensor("wo", [HPC, DH, D], F32, kind="ExternalInput").ap()
    gw_in = nc.dram_tensor("gw", [D, E], F32, kind="ExternalInput").ap()
    gb_in = nc.dram_tensor("gb", [1, E], F32, kind="ExternalInput").ap()
    w_in = [nc.dram_tensor(f"w{i}", [D, D], F32, kind="ExternalInput").ap() for i in (1, 2, 3)]
    b_in = [nc.dram_tensor(f"b{i}", [8, 128], F32, kind="ExternalInput").ap() for i in (1, 2, 3)]
    cos_in = nc.dram_tensor("cosT", [128, T], F32, kind="ExternalInput").ap()
    sgn_in = nc.dram_tensor("sgnT", [128, T], F32, kind="ExternalInput").ap()
    iden_in = nc.dram_tensor("iden", [128, 128], F32, kind="ExternalInput").ap()
    iden8_in = nc.dram_tensor("iden8", [8, 8], F32, kind="ExternalInput").ap()
    swap_in = nc.dram_tensor("swp64", [128, 128], F32, kind="ExternalInput").ap()
    mask_in = nc.dram_tensor("maskT", [128, 1024], F32, kind="ExternalInput").ap()
    sel_in = nc.dram_tensor("selrep", [8, 128], F32, kind="ExternalInput").ap()
    out_t = nc.dram_tensor("out", [2048, D], F32, kind="ExternalOutput").ap()

    xnscr = nc.dram_tensor("xnscr", [MC, 128, NTOK], F32).ap()
    kscr = nc.dram_tensor("kscr", [HPC, 128, T], F32).ap()
    qscr = nc.dram_tensor("qscr", [HPC, 128, T], F32).ap()
    vscr = nc.dram_tensor("vscr", [HPC, T, DH], F32).ap()
    lgtscr = nc.dram_tensor("lgtscr", [8, NTOK], F32).ap()
    wtscr = nc.dram_tensor("wtscr", [8, NTOK], F32).ap()
    rsin_t = nc.dram_tensor("rsin", [T, 1040], F32)
    rsout_t = nc.dram_tensor("rsout", [2048, 1040], F32)
    agvin_t = nc.dram_tensor("agvin", [8, 128], F32)
    agvout_t = nc.dram_tensor("agvout", [64, 128], F32)
    rsin, rsout = rsin_t.ap(), rsout_t.ap()
    agvout = agvout_t.ap()
    agvin = agvin_t.ap()

    cur = [0]
    XB = [_carve(cur, 4096), _carve(cur, 4096)]
    XF = [_carve(cur, 4096), _carve(cur, 4096)]
    WQ, WK, WV = _carve(cur, 2048), _carve(cur, 2048), _carve(cur, 2048)
    GW = _carve(cur, 64)
    TC = [_carve(cur, 512), _carve(cur, 512)]
    TS = [_carve(cur, 512), _carve(cur, 512)]
    RAW = [_carve(cur, 512), _carve(cur, 512)]
    TMA = [_carve(cur, 512), _carve(cur, 512)]
    TMB = [_carve(cur, 512), _carve(cur, 512)]
    VS1 = [_carve(cur, 512), _carve(cur, 512)]
    VS2 = [_carve(cur, 512), _carve(cur, 512)]
    LGS = [_carve(cur, 512), _carve(cur, 512)]
    STAT = _carve(cur, 64)
    SCR = _carve(cur, 1024)
    assert cur[0] <= 44544, cur[0]
    W1, W2, W3 = 0, 8192, 16384
    XNF = [24576, 28672]
    H1, H2 = 32768, 36864
    YB = [40960, 41472]
    WY = 41984
    WTSL = [42496, 43008]
    VPART = 43520
    assert VPART + 264 <= 44544
    LGB = [24576, 26624]
    EGB = 28672
    PRB = 30720
    PTM = 32768
    TK1, TK2 = 32896, 32912
    OH1, PM2, TMG, WTM, INDQ, W1T, W2T = 33024, 33152, 33280, 33408, 33536, 33664, 33792
    WTS = [33920, 34432]
    GACC = 34944
    GSUM = 34960
    INDPAD = 44000
    WO2 = 24576
    QB = [26624, 27136]
    KB = [27648, 28160]
    VB = [28672, 29184]
    EB = [29696, 30208, 30720]
    ONM = [31232, 31744]
    IZR = 32256
    IZV = 32768
    RPB = [33280, 33792]
    MSK = 35328
    assert MSK + 1024 <= 44544
    OUTB = [33088, 34112]
    VALL = 35136
    INDE = [36160, 36288]
    PERS = 44544
    IDEN = PERS
    IDEN8 = PERS + 128
    SELR = PERS + 136
    GB = PERS + 264
    B123 = PERS + 265
    SWP = PERS + 289
    assert SWP + 128 <= ARENA

    own = lambda t: t < NT // 2

    with contextlib.ExitStack() as es:
        ar_t = es.enter_context(nc.sbuf_tensor("arena", [128, ARENA], F32))
        ebf_t = [es.enter_context(nc.sbuf_tensor(f"ebf{i}", [128, 512], BF16)) for i in range(2)]
        onesb_t = es.enter_context(nc.sbuf_tensor("onesb", [128, 1], BF16))
        ones1_t = es.enter_context(nc.sbuf_tensor("ones1", [1, 128], F32))
        ps_t = [es.enter_context(nc.psum_tensor(f"ps{i}", [128, 512], F32)) for i in range(8)]
        sems = {}
        for s in ("s_sy", "s_gp", "s_pe", "s_act", "s_dve", "s_cc", "s_bar"):
            sems[s] = es.enter_context(nc.semaphore(s))
        block = es.enter_context(nc.Block())

        ar = ar_t.ap()
        ebf = [t.ap() for t in ebf_t]
        onesb = onesb_t.ap()
        ones1 = ones1_t.ap()
        ps = [t.ap() for t in ps_t]

        def A(col, w, p0=0, pn=128):
            return ar[p0:p0 + pn, col:col + w]

        def program(P: Walk):
            nbar = [0]

            def barrier():
                nbar[0] += 1
                sy_c, gp_c = P.cnt["s_sy"], P.cnt["s_gp"]
                for e in ENGS:
                    P.do(e, lambda g: g.nop(), inc="s_bar")
                for e in ENGS:
                    P.do(e, wait=[("s_bar", 5 * nbar[0]), ("s_sy", sy_c), ("s_gp", gp_c)])

            def ld(g, dst, src):
                return g.dma_start(out=dst, in_=src)

            P.do("sync", lambda g: [ld(g, A(IDEN, 128), iden_in[:, :]),
                                    ld(g, A(IDEN8, 8, 0, 8), iden8_in[:, :]),
                                    ld(g, A(SELR, 128, 0, 8), sel_in[:, :]),
                                    ld(g, A(SWP, 128), swap_in[:, :]),
                                    ld(g, A(MSK, 1024), mask_in[:, :])],
                 inc="s_sy", n=16 * 5)
            P.do("gpsimd", lambda g: [
                g.dma_start(out=A(GB, 1, 0, 8), in_=gb_in.transpose((1, 0)),
                            allow_slow_non_contiguous=True)] + [
                g.dma_start(out=A(B123 + 8 * i, 8), in_=b_in[i].transpose((1, 0)),
                            allow_slow_non_contiguous=True) for i in range(3)],
                 inc="s_gp", n=16 * 4)
            P.do("sync", lambda g: [
                ld(g, A(Wc + (h * 8 + mc) * 128, 128), w_ap[h, mc * 128:(mc + 1) * 128, :])
                for h in range(HPC) for mc in range(MC)
                for Wc, w_ap in ((WQ, wq_in), (WK, wk_in), (WV, wv_in))],
                 inc="s_sy", n=16 * 48)
            P.do("sync", lambda g: [ld(g, A(GW + 8 * mc, 8), gw_in[mc * 128:(mc + 1) * 128, :])
                                    for mc in range(MC)], inc="s_sy", n=16 * 8)
            P.do("vector", lambda v: v.memset(onesb[:, :], 1.0))
            P.do("vector", lambda v: v.memset(A(INDPAD, 256), 0.0))
            P.do("vector", lambda v: v.memset(ones1[:, :], 1.0))
            P.snap("consts", "s_sy")
            P.snap("consts_gp", "s_gp")

            aux_i = [0]
            aux_free = {}
            prj_i = [0]
            prj_free = {}

            def aux_bank():
                i = aux_i[0]
                aux_i[0] += 1
                w = [aux_free[i - 2]] if (i - 2) in aux_free else []
                return i, ps[i % 2], w

            def prj_bank():
                i = prj_i[0]
                prj_i[0] += 1
                w = [prj_free[i - 2]] if (i - 2) in prj_free else []
                return i, ps[3 + i % 2], w

            for t in range(NT):
                bt, p0 = t // (NT // 2), TT * (t % (NT // 2))
                gt0 = TT * t
                bf = t % 2
                sb = 32 * bf
                wl = [f"xbdone_{t-2}"] if t >= 2 else []
                P.do("sync", lambda g, bt=bt, p0=p0, bf=bf: [
                    ld(g, A(XB[bf] + 1024 * k, 1024), xin[bt, p0 + 128 * k: p0 + 128 * (k + 1), :])
                    for k in range(4)], inc="s_sy", n=64, wait=wl)
                P.snap(f"xload_{t}", "s_sy")
                if own(t):
                    wt = [f"ropedone_{t-2}"] if t >= 2 else []
                    P.do("sync", lambda g, p0=p0, bf=bf: [
                        ld(g, A(TC[bf], 512), cos_in[:, p0:p0 + 512]),
                        ld(g, A(TS[bf], 512), sgn_in[:, p0:p0 + 512])],
                        inc="s_sy", n=32, wait=wt)
                    P.snap(f"tload_{t}", "s_sy")
                P.do("scalar", lambda s, bf=bf, sb=sb: [
                    (s.activation(A(SCR, 1024), A(XB[bf] + 1024 * k, 1024), AF.Copy,
                                  accum_out=A(STAT + sb + k, 1)),
                     s.activation(A(SCR, 1024), A(XB[bf] + 1024 * k, 1024), AF.Square,
                                  accum_out=A(STAT + sb + 4 + k, 1)))
                    for k in range(4)][-1][-1], inc="s_act", wait=[f"xload_{t}"])
                P.snap(f"stats_{t}", "s_act")
                P.do("vector", lambda v, sb=sb: [
                    v.tensor_scalar(A(STAT + sb + 8, 4), A(STAT + sb, 4), 1.0 / D, None, OP.mult),
                    v.tensor_scalar(A(STAT + sb + 12, 4), A(STAT + sb + 4, 4), 1.0 / D, EPS, OP.mult, OP.add),
                ][-1], inc="s_dve", wait=[f"stats_{t}"])
                P.snap(f"lnstat0_{t}", "s_dve")
                P.do("vector", lambda v, sb=sb:
                     v.tensor_tensor(A(STAT + sb + 16, 4), A(STAT + sb + 8, 4), A(STAT + sb + 8, 4), OP.mult),
                     inc="s_dve", wait=[f"lnstat0_{t}"])
                P.snap(f"lnstat1_{t}", "s_dve")
                P.do("vector", lambda v, sb=sb:
                     v.tensor_tensor(A(STAT + sb + 20, 4), A(STAT + sb + 12, 4), A(STAT + sb + 16, 4), OP.subtract),
                     inc="s_dve", wait=[f"lnstat1_{t}"])
                P.snap(f"lnstat_{t}", "s_dve")
                P.do("scalar", lambda s, sb=sb: s.activation(A(STAT + sb + 24, 4), A(STAT + sb + 20, 4), AF.Sqrt),
                     inc="s_act", wait=[f"lnstat_{t}"])
                P.snap(f"sden_{t}", "s_act")
                P.do("vector", lambda v, sb=sb: v.reciprocal(A(STAT + sb + 28, 4), A(STAT + sb + 24, 4)),
                     inc="s_dve", wait=[f"sden_{t}"])
                P.snap(f"rstd0_{t}", "s_dve")
                P.do("vector", lambda v, sb=sb:
                     v.tensor_tensor(A(STAT + sb + 16, 4), A(STAT + sb + 28, 4), A(STAT + sb + 28, 4), OP.mult),
                     inc="s_dve", wait=[f"rstd0_{t}"])
                P.snap(f"nt1_{t}", "s_dve")
                P.do("vector", lambda v, sb=sb:
                     v.tensor_tensor(A(STAT + sb + 16, 4), A(STAT + sb + 20, 4), A(STAT + sb + 16, 4), OP.mult),
                     inc="s_dve", wait=[f"nt1_{t}"])
                P.snap(f"nt2_{t}", "s_dve")
                P.do("vector", lambda v, sb=sb:
                     v.tensor_scalar(A(STAT + sb + 16, 4), A(STAT + sb + 16, 4), -0.5, 1.5, OP.mult, OP.add),
                     inc="s_dve", wait=[f"nt2_{t}"])
                P.snap(f"nt3_{t}", "s_dve")
                P.do("vector", lambda v, sb=sb:
                     v.tensor_tensor(A(STAT + sb + 28, 4), A(STAT + sb + 28, 4), A(STAT + sb + 16, 4), OP.mult),
                     inc="s_dve", wait=[f"nt3_{t}"])
                P.snap(f"rstd_{t}", "s_dve")
                P.do("vector", lambda v, sb=sb, bf=bf: [
                    v.tensor_scalar(A(XB[bf] + 1024 * k, 1024), A(XB[bf] + 1024 * k, 1024),
                                    A(STAT + sb + 8 + k, 1), A(STAT + sb + 28 + k, 1),
                                    OP.subtract, OP.mult) for k in range(4)][-1],
                     inc="s_dve", wait=[f"rstd_{t}"])
                P.snap(f"xn_{t}", "s_dve")
                for mc in range(MC):
                    ai, bank, wfree = aux_bank()
                    P.do("tensor", lambda te, bf=bf, mc=mc, bank=bank: [
                        te.transpose(bank[:, 128 * k:128 * (k + 1)],
                                     A(XB[bf] + 1024 * k + 128 * mc, 128), A(IDEN, 128))
                        for k in range(4)][-1], inc="s_pe",
                         wait=[f"xn_{t}", "consts"] + wfree)
                    P.snap(f"tr_{t}_{mc}", "s_pe")
                    wx = ([f"xfdone_{t-2}", f"xfpe_{t-2}"] if (t >= 2 and mc == 0) else [])
                    P.do("scalar", lambda s, bf=bf, mc=mc, bank=bank:
                         s.activation(A(XF[bf] + 512 * mc, 512), bank[:, 0:512], AF.Copy),
                         inc="s_act", wait=[f"tr_{t}_{mc}"] + wx)
                    P.snap(f"xfev_{t}_{mc}", "s_act")
                    aux_free[ai] = f"xfev_{t}_{mc}"
                for mc in range(MC):
                    wg = [f"lgev_{t-1}"] if (t >= 1 and mc == 0) else []
                    P.do("tensor", lambda te, bf=bf, mc=mc:
                         te.matmul(ps[2][0:8, 0:512], A(GW + 8 * mc, 8), A(XF[bf] + 512 * mc, 512),
                                   start=(mc == 0), stop=(mc == MC - 1)),
                         inc="s_pe" if mc == MC - 1 else None,
                         wait=[f"xfev_{t}_{mc}"] + wg)
                P.snap(f"gmm_{t}", "s_pe")
                wlg = [f"lgw_{t-2}"] if t >= 2 else []
                P.do("scalar", lambda s, bf=bf: s.activation(A(LGS[bf], 512, 0, 8), ps[2][0:8, 0:512],
                                                             AF.Identity, bias=A(GB, 1, 0, 8)),
                     inc="s_act", wait=[f"gmm_{t}", "consts_gp"] + wlg)
                P.snap(f"lgev_{t}", "s_act")
                P.do("gpsimd", lambda g, bf=bf, gt0=gt0: [g.dma_start(out=lgtscr[:, gt0:gt0 + 512],
                                                                      in_=A(LGS[bf], 512, 0, 8))],
                     inc="s_gp", n=16, wait=[f"lgev_{t}"])
                P.snap(f"lgw_{t}", "s_gp")
                if own(t):
                    for h in range(HPC):
                        for X, Wc in (("k", WK), ("v", WV), ("q", WQ)):
                            pi, pbank, wfree = prj_bank()
                            for mc in range(MC):
                                P.do("tensor", lambda te, h=h, mc=mc, Wc=Wc, pbank=pbank, bf=bf:
                                     te.matmul(pbank[:, 0:512], A(Wc + (h * 8 + mc) * 128, 128),
                                               A(XF[bf] + 512 * mc, 512),
                                               start=(mc == 0), stop=(mc == MC - 1)),
                                     inc="s_pe" if mc == MC - 1 else None,
                                     wait=[f"xfev_{t}_{mc}"] + (wfree if mc == 0 else []))
                            P.snap(f"pmm_{t}_{h}_{X}", "s_pe")
                            if X == "v":
                                wv2 = [f"vw_{t-2}_{h}"] if t >= 2 else []
                                P.do("scalar", lambda s, bf=bf, pbank=pbank:
                                     s.activation(A(VS1[bf], 512), pbank[:, 0:512], AF.Copy),
                                     inc="s_act", wait=[f"pmm_{t}_{h}_{X}"] + wv2)
                                P.snap(f"vev1_{t}_{h}", "s_act")
                                prj_free[pi] = f"vev1_{t}_{h}"
                                ai, bank, wfree2 = aux_bank()
                                P.do("tensor", lambda te, bf=bf, bank=bank: [
                                    te.transpose(bank[:, 128 * k:128 * (k + 1)],
                                                 A(VS1[bf] + 128 * k, 128), A(IDEN, 128))
                                    for k in range(4)][-1], inc="s_pe",
                                     wait=[f"vev1_{t}_{h}"] + wfree2)
                                P.snap(f"vtr_{t}_{h}", "s_pe")
                                P.do("scalar", lambda s, bf=bf, bank=bank:
                                     s.activation(A(VS2[bf], 512), bank[:, 0:512], AF.Copy),
                                     inc="s_act", wait=[f"vtr_{t}_{h}"])
                                P.snap(f"vev2_{t}_{h}", "s_act")
                                aux_free[ai] = f"vev2_{t}_{h}"
                                P.do("gpsimd", lambda g, bf=bf, h=h, p0=p0: [
                                    g.dma_start(out=vscr[h, p0 + 128 * k:p0 + 128 * (k + 1), :],
                                                in_=A(VS2[bf] + 128 * k, 128))
                                    for k in range(4)], inc="s_gp", n=64,
                                     wait=[f"vev2_{t}_{h}"])
                                P.snap(f"vw_{t}_{h}", "s_gp")
                            else:
                                wr2 = [f"{X}w_{t-2}_{h}"] if t >= 2 else []
                                P.do("scalar", lambda s, bf=bf, pbank=pbank:
                                     s.activation(A(RAW[bf], 512), pbank[:, 0:512], AF.Copy),
                                     inc="s_act", wait=[f"pmm_{t}_{h}_{X}"] + wr2)
                                P.snap(f"rawev_{t}_{h}_{X}", "s_act")
                                prj_free[pi] = f"rawev_{t}_{h}_{X}"
                                ai, bank, wfree2 = aux_bank()
                                P.do("tensor", lambda te, bf=bf, bank=bank:
                                     te.matmul(bank[:, 0:512], A(SWP, 128), A(RAW[bf], 512),
                                               start=True, stop=True),
                                     inc="s_pe", wait=[f"rawev_{t}_{h}_{X}"] + wfree2)
                                P.snap(f"swp_{t}_{h}_{X}", "s_pe")
                                P.do("vector", lambda v, bf=bf, bank=bank: [
                                    v.tensor_tensor(A(TMA[bf], 512), A(RAW[bf], 512), A(TC[bf], 512), OP.mult),
                                    v.tensor_tensor(A(TMB[bf], 512), bank[:, 0:512], A(TS[bf], 512), OP.mult),
                                ][-1], inc="s_dve",
                                     wait=[f"swp_{t}_{h}_{X}", f"tload_{t}"])
                                P.snap(f"ropem_{t}_{h}_{X}", "s_dve")
                                P.do("vector", lambda v, bf=bf:
                                     v.tensor_tensor(A(RAW[bf], 512), A(TMA[bf], 512), A(TMB[bf], 512), OP.add),
                                     inc="s_dve", wait=[f"ropem_{t}_{h}_{X}"])
                                P.snap(f"rope_{t}_{h}_{X}", "s_dve")
                                aux_free[ai] = f"rope_{t}_{h}_{X}"
                                scr_ap = kscr if X == "k" else qscr
                                P.do("gpsimd", lambda g, bf=bf, h=h, p0=p0, scr_ap=scr_ap:
                                     [g.dma_start(out=scr_ap[h, :, p0:p0 + 512], in_=A(RAW[bf], 512))],
                                     inc="s_gp", n=16, wait=[f"rope_{t}_{h}_{X}"])
                                P.snap(f"{X}w_{t}_{h}", "s_gp")
                    P.snap(f"ropedone_{t}", "s_dve")
                P.do("gpsimd", lambda g, bf=bf, gt0=gt0: [
                    g.dma_start(out=xnscr[mc][:, gt0:gt0 + 512], in_=A(XF[bf] + 512 * mc, 512))
                    for mc in range(MC)], inc="s_gp", n=16 * MC,
                     wait=[f"xfev_{t}_{MC-1}"])
                P.snap(f"xnw_{t}", "s_gp")
                P.snap(f"xbdone_{t}", "s_pe")
                P.snap(f"xfdone_{t}", "s_gp")
                P.snap(f"xfpe_{t}", "s_pe")

            barrier()

            for ch in range(NCH):
                wl = [f"gexp1_{ch-2}"] if ch >= 2 else []
                P.do("sync", lambda g, ch=ch: [ld(g, A(LGB[ch % 2], 2048, 0, 8),
                                                  lgtscr[:, CH * ch:CH * (ch + 1)])],
                     inc="s_sy", n=16, wait=wl)
                P.snap(f"glgl_{ch}", "s_sy")
                P.do("scalar", lambda s, ch=ch: s.activation(A(EGB, 2048, 0, 8), A(LGB[ch % 2], 2048, 0, 8),
                                                             AF.Exp, accum_out=A(GACC + ch, 1, 0, 8)),
                     inc="s_act", wait=[f"glgl_{ch}"])
                P.snap(f"gexp1_{ch}", "s_act")
            P.do("vector", lambda v: v.reduce_sum(A(GSUM, 2, 0, 8),
                                                   A(GACC, 8, 0, 8).rearrange("p (b c) -> p b c", c=4),
                                                   axis=AX.X),
                 inc="s_dve", wait=[f"gexp1_{NCH-1}"])
            P.snap("gsum", "s_dve")
            P.do("vector", lambda v: v.reciprocal(A(GSUM + 2, 2, 0, 8), A(GSUM, 2, 0, 8)),
                 inc="s_dve", wait=["gsum"])
            P.snap("ginv", "s_dve")
            for ch in range(NCH):
                bt = ch // 4
                P.do("sync", lambda g, ch=ch: [ld(g, A(LGB[ch % 2], 2048, 0, 8),
                                                  lgtscr[:, CH * ch:CH * (ch + 1)])],
                     inc="s_sy", n=16, wait=[f"gexp2_{ch-2}"] if ch >= 2 else ["ginv"])
                P.snap(f"glg2_{ch}", "s_sy")
                P.do("scalar", lambda s, ch=ch: s.activation(A(EGB, 2048, 0, 8), A(LGB[ch % 2], 2048, 0, 8), AF.Exp),
                     inc="s_act", wait=[f"glg2_{ch}", (f"gtr_{ch-1}" if ch >= 1 else "ginv")])
                P.snap(f"gexp2_{ch}", "s_act")
                P.do("vector", lambda v, bt=bt: v.tensor_scalar(A(PRB, 2048, 0, 8), A(EGB, 2048, 0, 8),
                                                                A(GSUM + 2 + bt, 1, 0, 8), None, OP.mult),
                     inc="s_dve", wait=[f"gexp2_{ch}", "ginv"])
                P.snap(f"gprb_{ch}", "s_dve")
                P.do("tensor", lambda te, ch=ch: [
                    te.matmul(ps[ch % 2][:, 8 * blk:8 * (blk + 1)], A(PRB + 128 * blk, 128, 0, 8),
                              A(IDEN8, 8, 0, 8), start=True, stop=True)
                    for blk in range(16)][-1], inc="s_pe",
                     wait=[f"gprb_{ch}"] + ([f"gtk_{ch-2}"] if ch >= 2 else []))
                P.snap(f"gtr_{ch}", "s_pe")
                P.do("scalar", lambda s, ch=ch: s.activation(A(PTM, 128), ps[ch % 2][:, 0:128], AF.Copy),
                     inc="s_act", wait=[f"gtr_{ch}"])
                P.snap(f"gptm_{ch}", "s_act")
                p3 = A(PTM, 128).rearrange("p (g e) -> p g e", e=8)

                t1b = A(TK1, 16).unsqueeze(-1).broadcast_to((128, 16, 8))
                t2b = A(TK2, 16).unsqueeze(-1).broadcast_to((128, 16, 8))
                o1_3 = A(OH1, 128).rearrange("p (g e) -> p g e", e=8)
                o2_3 = A(PM2, 128).rearrange("p (g e) -> p g e", e=8)
                pm3 = A(TMG, 128).rearrange("p (g e) -> p g e", e=8)
                wtk = [f"gptm_{ch}"]
                if 0 < ch <= 4:
                    wtk.append(f"gindw_{ch-1}")
                steps = [
                    lambda v: v.reduce_max(A(TK1, 16), p3, axis=AX.X),
                    lambda v: v.tensor_tensor(o1_3, p3, t1b, OP.is_equal),
                    lambda v: v.tensor_scalar(A(W1T, 128), A(OH1, 128), 1e30, None, OP.mult),
                    lambda v: v.tensor_tensor(A(TMG, 128), A(PTM, 128), A(W1T, 128), OP.subtract),
                    lambda v: v.reduce_max(A(TK2, 16), pm3, axis=AX.X),
                    lambda v: v.tensor_tensor(o2_3, pm3, t2b, OP.is_equal),
                    lambda v: v.tensor_tensor(A(W1T, 128).rearrange("p (g e) -> p g e", e=8), o1_3, t1b, OP.mult),
                    lambda v: v.tensor_tensor(A(W2T, 128).rearrange("p (g e) -> p g e", e=8), o2_3, t2b, OP.mult),
                    lambda v: v.tensor_tensor(A(WTM, 128), A(W1T, 128), A(W2T, 128), OP.add),
                    lambda v: v.tensor_tensor(A(INDQ, 128), A(OH1, 128), A(PM2, 128), OP.add),
                    lambda v: v.tensor_scalar(A(INDQ, 128), A(INDQ, 128), 0.25, None, OP.mult),
                ]
                for si, st_fn in enumerate(steps):
                    P.do("vector", st_fn, inc="s_dve",
                         wait=(wtk if si == 0 else [f"gtk{si-1}_{ch}"]))
                    P.snap(f"gtk{si}_{ch}", "s_dve")
                P.snap(f"gtk_{ch}", "s_dve")
                if ch < 4:
                    P.do("vector", lambda v: [
                        v.tensor_copy(A(INDPAD + 16 * blk, 8), A(INDQ + 8 * blk, 8))
                        for blk in range(16)][-1], inc="s_dve", wait=[f"gtk_{ch}"])
                    P.snap(f"gindp_{ch}", "s_dve")
                for g4 in range(4):
                    P.do("tensor", lambda te, g4=g4: [
                        te.matmul(ps[2 + g4 % 2][0:8, 128 * q:128 * (q + 1)],
                                  A(WTM + 8 * (4 * g4 + q), 8), A(IDEN, 128), start=True, stop=True)
                        for q in range(4)][-1], inc="s_pe",
                         wait=[f"gtk_{ch}"] + ([f"gwev_{ch}_{g4-2}"] if g4 >= 2
                                               else ([f"gwev_{ch-1}_{g4+2}"] if ch >= 1 else [])))
                    P.snap(f"gwtr_{ch}_{g4}", "s_pe")
                    P.do("scalar", lambda s, g4=g4: s.activation(A(WTS[g4 % 2], 512, 0, 8),
                                                                 ps[2 + g4 % 2][0:8, 0:512], AF.Copy),
                         inc="s_act", wait=[f"gwtr_{ch}_{g4}"] + ([f"gww_{ch}_{g4-2}"] if g4 >= 2
                                                                  else ([f"gww_{ch-1}_{g4+2}"] if ch >= 1 else [])))
                    P.snap(f"gwev_{ch}_{g4}", "s_act")
                    P.do("gpsimd", lambda g, ch=ch, g4=g4: [g.dma_start(
                        out=wtscr[:, CH * ch + 512 * g4: CH * ch + 512 * (g4 + 1)],
                        in_=A(WTS[g4 % 2], 512, 0, 8))], inc="s_gp", n=16, wait=[f"gwev_{ch}_{g4}"])
                    P.snap(f"gww_{ch}_{g4}", "s_gp")
                if ch < 4:
                    P.do("gpsimd", lambda g, ch=ch: [
                        g.dma_start(out=rsin[CH * ch + 128 * blk: CH * ch + 128 * (blk + 1), 1024:1040],
                                    in_=A(INDPAD + 16 * blk, 16))
                        for blk in range(16)], inc="s_gp", n=16 * 16, wait=[f"gindp_{ch}"])
                    P.snap(f"gindw_{ch}", "s_gp")

            barrier()

            P.do("sync", lambda g: [
                ld(g, A(Wc + 1024 * mc, 1024), w_ap[128 * mc:128 * (mc + 1), :])
                for Wc, w_ap in ((W1, w_in[0]), (W2, w_in[1]), (W3, w_in[2]))
                for mc in range(MC)] + [
                ld(g, A(WO2 + 1024 * h, 1024), wo_in[h, :, :]) for h in range(HPC)],
                 inc="s_sy", n=16 * 26)
            P.snap("wffn", "s_sy")

            for it in range(NIT):
                i0 = 512 * it
                for h in range(HPC):
                    a = 2 * it + h
                    P.do("sync", lambda g, h=h, i0=i0, a=a: [ld(g, A(QB[a % 2], 512), qscr[h, :, i0:i0 + 512])],
                         inc="s_sy", n=16, wait=[f"apvlast_{a-2}"] if a >= 2 else [])
                    P.snap(f"aql_{it}_{h}", "s_sy")
                    units = [(jc, js) for jc in range(it + 1) for js in range(4)]
                    nu = len(units)

                    def kvload(jc, wv):
                        P.do("sync", lambda g, h=h, jc=jc: [
                            ld(g, A(KB[jc % 2], 512), kscr[h, :, 512 * jc:512 * (jc + 1)]),
                            ld(g, A(VB[jc % 2], 512).rearrange("p (s d) -> p s d", s=4),
                               vscr[h, 512 * jc:512 * (jc + 1), :].rearrange("(s p) d -> p s d", p=128)),
                        ], inc="s_sy", n=32, wait=wv)
                        P.snap(f"akvl_{it}_{h}_{jc}", "s_sy")

                    for jc in range(min(2, it + 1)):
                        kvload(jc, [f"apvlast_{a-1}"] if a >= 1 else [])
                    for u in range(nu + 1):
                        if u < nu:
                            jc, js = units[u]
                            diag = (jc == it)
                            wS = [f"akvl_{it}_{h}_{jc}", f"aql_{it}_{h}"]
                            if u >= 2:
                                wS.append(f"aexp_{it}_{h}_{u-2}")
                            elif a >= 1:
                                wS.append(f"aexpl_{a-1}")
                            P.do("tensor", lambda te, jc=jc, js=js, u=u, a=a:
                                 te.matmul(ps[u % 2][:, 0:512], A(KB[jc % 2] + 128 * js, 128),
                                           A(QB[a % 2], 512), start=True, stop=True),
                                 inc="s_pe", wait=wS)
                            P.snap(f"asmm_{it}_{h}_{u}", "s_pe")
                            if diag:
                                d = 128 * js
                                P.do("vector", lambda v, u=u, d=d:
                                     v.tensor_tensor(ps[u % 2][:, 0:512], ps[u % 2][:, 0:512],
                                                     A(MSK + 512 - d, 512), OP.add),
                                     inc="s_dve", wait=[f"asmm_{it}_{h}_{u}"])
                                P.snap(f"amsk_{it}_{h}_{u}", "s_dve")
                        if u > 0:
                            up = u - 1
                            jcp, jsp = units[up]
                            diagp = (jcp == it)
                            wE = [f"amsk_{it}_{h}_{up}" if diagp else f"asmm_{it}_{h}_{up}"]
                            if up >= 3:
                                wE.append(f"apv_{it}_{h}_{up-3}")
                            P.do("scalar", lambda s, up=up: s.activation(A(EB[up % 3], 512),
                                                                         ps[up % 2][:, 0:512],
                                                                         AF.Exp, scale=1.0 / 128.0),
                                 inc="s_act", wait=wE)
                            P.snap(f"aexp_{it}_{h}_{up}", "s_act")
                            if up == nu - 1:
                                P.snap(f"aexpl_{a}", "s_act")
                            wC = [f"aexp_{it}_{h}_{up}"]
                            if up >= 2:
                                wC.append(f"az_{it}_{h}_{up-2}")
                            P.do("vector", lambda v, up=up: v.tensor_copy(ebf[up % 2][:, :], A(EB[up % 3], 512)),
                                 inc="s_dve", wait=wC)
                            P.snap(f"acst_{it}_{h}_{up}", "s_dve")
                            wP = [f"aexp_{it}_{h}_{up}"]
                            if up == 0 and a >= 2:
                                wP.append(f"aofree_{a-2}")
                            P.do("tensor", lambda te, jcp=jcp, jsp=jsp, up=up, a=a:
                                 te.matmul(ps[2 + a % 2][:, 0:512], A(VB[jcp % 2] + 128 * jsp, 128),
                                           A(EB[up % 3], 512), start=(up == 0), stop=(up == nu - 1)),
                                 inc="s_pe", wait=wP)
                            P.snap(f"apv_{it}_{h}_{up}", "s_pe")
                            wZ = [f"acst_{it}_{h}_{up}"]
                            if up == 0 and a >= 2:
                                wZ.append(f"azfree_{a-2}")
                            P.do("tensor", lambda te, up=up, a=a:
                                 te.matmul(ps[4 + a % 2][0:1, 0:512], onesb[:, :], ebf[up % 2][:, :],
                                           start=(up == 0), stop=(up == nu - 1)),
                                 inc="s_pe", wait=wZ)
                            P.snap(f"az_{it}_{h}_{up}", "s_pe")
                            if u % 4 == 0 and u > 0:
                                jcl = u // 4 + 1
                                if jcl <= it:
                                    kvload(jcl, [f"apv_{it}_{h}_{4*jcl-5}"])
                    P.snap(f"apvlast_{a}", "s_pe")
                    P.do("vector", lambda v, a=a: v.reciprocal(A(IZV, 512, 0, 1), ps[4 + a % 2][0:1, 0:512]),
                         inc="s_dve", wait=[f"az_{it}_{h}_{nu-1}"])
                    P.snap(f"aivz_{it}_{h}", "s_dve")
                    P.snap(f"azfree_{a}", "s_dve")
                    wrep = [f"aivz_{it}_{h}"]
                    if h == 1:
                        wrep.append(f"aonm_{it}_0")
                    elif it >= 1:
                        wrep.append(f"aonm_{it-1}_1")
                        wrep.append(f"arev_{it-1}_6")
                    P.do("tensor", lambda te: te.matmul(ps[6][:, 0:512], ones1[:, :], A(IZV, 512, 0, 1),
                                                        start=True, stop=True),
                         inc="s_pe", wait=wrep)
                    P.snap(f"arep_{it}_{h}", "s_pe")
                    P.do("vector", lambda v: v.tensor_copy(A(IZR, 512), ps[6][:, 0:512]),
                         inc="s_dve", wait=[f"arep_{it}_{h}"])
                    P.snap(f"aizr_{it}_{h}", "s_dve")
                    P.do("vector", lambda v, h=h, a=a:
                         v.tensor_tensor(A(ONM[h], 512), ps[2 + a % 2][:, 0:512], A(IZR, 512), OP.mult),
                         inc="s_dve", wait=[f"aizr_{it}_{h}", f"apv_{it}_{h}_{nu-1}"])
                    P.snap(f"aonm_{it}_{h}", "s_dve")
                    P.snap(f"aofree_{a}", "s_dve")
                r_i = 0
                for ib in range(4):
                    for mh in range(2):
                        wR = [f"aonm_{it}_1", "wffn"]
                        if r_i >= 2:
                            wR.append(f"arev_{it}_{r_i-2}")
                        elif it >= 1:
                            wR.append(f"arev_{it-1}_{r_i+6}")
                        P.do("tensor", lambda te, ib=ib, mh=mh, r_i=r_i: [
                            te.matmul(ps[6 + r_i % 2][:, 0:512], A(ONM[h2] + 128 * ib, 128),
                                      A(WO2 + 1024 * h2 + 512 * mh, 512),
                                      start=(h2 == 0), stop=(h2 == 1))
                            for h2 in range(HPC)][-1], inc="s_pe", wait=wR)
                        P.snap(f"armm_{it}_{r_i}", "s_pe")
                        P.do("scalar", lambda s, r_i=r_i: s.activation(A(RPB[r_i % 2], 512),
                                                                       ps[6 + r_i % 2][:, 0:512], AF.Copy),
                             inc="s_act", wait=[f"armm_{it}_{r_i}"] +
                             ([f"arw_{it}_{r_i-2}"] if r_i >= 2 else
                              ([f"arw_{it-1}_{r_i+6}"] if it >= 1 else [])))
                        P.snap(f"arev_{it}_{r_i}", "s_act")
                        P.do("gpsimd", lambda g, it=it, ib=ib, mh=mh, r_i=r_i: [g.dma_start(
                            out=rsin[512 * it + 128 * ib: 512 * it + 128 * (ib + 1), 512 * mh:512 * (mh + 1)],
                            in_=A(RPB[r_i % 2], 512))], inc="s_gp", n=16, wait=[f"arev_{it}_{r_i}"])
                        P.snap(f"arw_{it}_{r_i}", "s_gp")
                        r_i += 1
            P.do("gpsimd", lambda g: g.collective_compute(
                "ReduceScatter", OP.add, replica_groups=[[0, 1, 2, 3], [4, 5, 6, 7]],
                ins=[rsin_t.ap().opt()], outs=[rsout_t.ap().opt()]),
                 inc="s_cc", wait=[("s_gp", P.cnt["s_gp"])])

            barrier()

            for ft in range(NFT):
                t0 = 512 * ft
                fb = ft % 2
                P.do("sync", lambda g, fb=fb, t0=t0: [
                    ld(g, A(XNF[fb] + 512 * mc, 512), xnscr[mc][:, t0:t0 + 512])
                    for mc in range(MC)], inc="s_sy", n=16 * MC,
                     wait=[f"fl1_{ft-2}_done"] if ft >= 2 else [])
                P.snap(f"fxl_{ft}", "s_sy")
                P.do("sync", lambda g, fb=fb, t0=t0: [ld(g, A(WTSL[fb], 512, 0, 8), wtscr[:, t0:t0 + 512])],
                     inc="s_sy", n=16, wait=[f"fwr_{ft-2}"] if ft >= 2 else [])
                P.snap(f"fwtl_{ft}", "s_sy")
                P.do("tensor", lambda te, fb=fb: te.matmul(ps[2 + fb][:, 0:512], A(SELR, 128, 0, 8),
                                                           A(WTSL[fb], 512, 0, 8), start=True, stop=True),
                     inc="s_pe", wait=[f"fwtl_{ft}", "consts"] +
                     ([f"fvp_{ft-2}_7"] if ft >= 2 else []))
                P.snap(f"fwr_{ft}", "s_pe")
                for li, (Wc, src, dst) in enumerate(((W1, None, H1), (W2, H1, H2), (W3, H2, None))):
                    for oc in range(MC):
                        f_i = li * MC + oc
                        for mc in range(MC):
                            wmm = []
                            if src is None:
                                rhs = A(XNF[fb] + 512 * mc, 512)
                                if mc == 0:
                                    wmm += [f"fxl_{ft}", "wffn"]
                            else:
                                rhs = A(src + 512 * mc, 512)
                                wmm.append(f"fev_{ft}_{li-1}_{mc}")
                            if mc == 0:
                                if f_i >= 2:
                                    wmm.append(f"fev_{ft}_{li}_{oc-2}" if oc >= 2 else
                                               f"fev_{ft}_{li-1}_{oc+6}")
                                elif ft >= 1:
                                    wmm.append(f"fev_{ft-1}_2_{oc+6}")
                            P.do("tensor", lambda te, Wc=Wc, oc=oc, mc=mc, rhs=rhs, f_i=f_i:
                                 te.matmul(ps[f_i % 2][:, 0:512], A(Wc + 1024 * mc + 128 * oc, 128), rhs,
                                           start=(mc == 0), stop=(mc == MC - 1)),
                                 inc="s_pe" if mc == MC - 1 else None, wait=wmm)
                        P.snap(f"fmm_{ft}_{li}_{oc}", "s_pe")
                        if li == 0 and oc == MC - 1:
                            P.snap(f"fl1_{ft}_done", "s_pe")
                        if dst is not None:
                            wev = [f"fmm_{ft}_{li}_{oc}"]
                            if ft >= 1:
                                wev.append(f"fmm_{ft-1}_{li+1}_7")
                            P.do("scalar", lambda s, li=li, oc=oc, dst=dst, f_i=f_i:
                                 s.activation(A(dst + 512 * oc, 512), ps[f_i % 2][:, 0:512], AF.Relu,
                                              bias=A(B123 + 8 * li + oc, 1)),
                                 inc="s_act", wait=wev)
                            P.snap(f"fev_{ft}_{li}_{oc}", "s_act")
                        else:
                            yb = oc % 2
                            wy = [f"fmm_{ft}_{li}_{oc}"]
                            if oc >= 2:
                                wy.append(f"fvp_{ft}_{oc-2}")
                            elif ft >= 1:
                                wy.append(f"fvp_{ft-1}_{oc+6}")
                            P.do("scalar", lambda s, yb=yb, f_i=f_i, oc=oc:
                                 s.activation(A(YB[yb], 512), ps[f_i % 2][:, 0:512], AF.Relu,
                                              bias=A(B123 + 16 + oc, 1)),
                                 inc="s_act", wait=wy)
                            P.snap(f"fye_{ft}_{oc}", "s_act")
                            P.snap(f"fev_{ft}_{li}_{oc}", "s_act")
                            P.do("vector", lambda v, yb=yb, fb=fb:
                                 v.tensor_tensor(A(WY, 512), A(YB[yb], 512), ps[2 + fb][:, 0:512], OP.mult),
                                 inc="s_dve", wait=[f"fye_{ft}_{oc}", f"fwr_{ft}"])
                            P.snap(f"fwy_{ft}_{oc}", "s_dve")
                            P.do("vector", lambda v, oc=oc, ft=ft:
                                 v.reduce_sum(A(VPART + 32 * oc + ft, 1), A(WY, 512), axis=AX.X),
                                 inc="s_dve", wait=[f"fwy_{ft}_{oc}"])
                            P.snap(f"fvp_{ft}_{oc}", "s_dve")
            P.do("vector", lambda v: v.reduce_sum(A(VPART + 256, 8),
                                                  A(VPART, 256).rearrange("p (o f) -> p o f", f=32),
                                                  axis=AX.X),
                 inc="s_dve", wait=[f"fvp_{NFT-1}_7"])
            P.snap("fvsum", "s_dve")
            P.do("tensor", lambda te: te.matmul(ps[4][0:8, 0:128], A(VPART + 256, 8), A(IDEN, 128),
                                                start=True, stop=True),
                 inc="s_pe", wait=["fvsum", "consts"])
            P.snap("fvt", "s_pe")
            P.do("scalar", lambda s: s.activation(A(44300, 128, 0, 8), ps[4][0:8, 0:128], AF.Copy),
                 inc="s_act", wait=["fvt"])
            P.snap("fve", "s_act")
            P.do("gpsimd", lambda g: [g.dma_start(out=agvin[:, :], in_=A(44300, 128, 0, 8))],
                 inc="s_gp", n=16, wait=["fve"])
            P.do("gpsimd", lambda g: g.collective_compute(
                "AllGather", OP.bypass, replica_groups=[list(range(8))],
                ins=[agvin_t.ap().opt()], outs=[agvout_t.ap().opt()]),
                 inc="s_cc", wait=[("s_gp", P.cnt["s_gp"])])

            barrier()

            RSA, XRA = 0, 16640
            P.do("sync", lambda g: [
                ld(g, A(VALL + 128 * mc, 128, 0, 8),
                   agvout.rearrange("(e c) p -> e c p", c=8)[:, mc, :])
                for mc in range(MC)], inc="s_sy", n=16 * MC, wait=[("s_cc", 2)])
            P.snap("vl", "s_sy")
            P.do("sync", lambda g: [
                ld(g, ar[:, RSA:RSA + 16640].rearrange("p (g c) -> p g c", c=1040),
                   rsout.rearrange("(g p) c -> p g c", p=128)),
                ld(g, ar[:, XRA:XRA + 16384].rearrange("p (g c) -> p g c", c=1024),
                   xres_in.rearrange("(g p) c -> p g c", p=128)),
            ], inc="s_sy", n=32, wait=[("s_cc", 2)])
            P.snap("rsl", "s_sy")
            for ib in range(16):
                r0 = 128 * ib
                bb = ib % 2
                rb = RSA + 1040 * ib
                xb_ = XRA + 1024 * ib
                P.do("tensor", lambda te, rb=rb: te.matmul(ps[4][0:8, 0:128], A(rb + 1024, 8),
                                                           A(IDEN, 128), start=True, stop=True),
                     inc="s_pe", wait=["rsl", "consts"] + ([f"oie_{ib-1}"] if ib >= 1 else []))
                P.snap(f"oit_{ib}", "s_pe")
                P.do("scalar", lambda s, bb=bb: s.activation(A(INDE[bb], 128, 0, 8), ps[4][0:8, 0:128], AF.Copy),
                     inc="s_act", wait=[f"oit_{ib}"])
                P.snap(f"oie_{ib}", "s_act")
                for mh in range(2):
                    P.do("tensor", lambda te, bb=bb, mh=mh: te.matmul(ps[mh][:, 0:512], A(INDE[bb], 128, 0, 8),
                                                                      A(VALL + 512 * mh, 512, 0, 8),
                                                                      start=True, stop=True),
                         inc="s_pe", wait=[f"oie_{ib}", "vl"] + ([f"oou_{ib-1}"] if ib >= 1 else []))
                    P.snap(f"ofm_{ib}_{mh}", "s_pe")
                P.do("vector", lambda v, bb=bb, rb=rb: [
                    v.tensor_tensor(A(OUTB[bb] + 512 * mh, 512), ps[mh][:, 0:512],
                                    A(rb + 512 * mh, 512), OP.add)
                    for mh in range(2)][-1], inc="s_dve",
                     wait=[f"ofm_{ib}_0", f"ofm_{ib}_1"] +
                     ([f"oow_{ib-2}"] if ib >= 2 else []))
                P.snap(f"oou1_{ib}", "s_dve")
                P.do("vector", lambda v, bb=bb, xb_=xb_: [
                    v.tensor_tensor(A(OUTB[bb] + 512 * mh, 512), A(OUTB[bb] + 512 * mh, 512),
                                    A(xb_ + 512 * mh, 512), OP.add)
                    for mh in range(2)][-1], inc="s_dve",
                     wait=[f"oou1_{ib}"])
                P.snap(f"oou_{ib}", "s_dve")
                P.do("gpsimd", lambda g, bb=bb, r0=r0: [g.dma_start(out=out_t[r0:r0 + 128, :],
                                                                    in_=A(OUTB[bb], 1024))],
                     inc="s_gp", n=16, wait=[f"oou_{ib}"])
                P.snap(f"oow_{ib}", "s_gp")
            P.do("gpsimd", wait=[("s_gp", P.cnt["s_gp"])])

        for engname in ENGS:
            def mk(name):
                def _(eng):
                    program(Walk(name, eng, sems))
                return _
            getattr(block, engname)(mk(engname))

    return nc


_NC_CACHE = []


def kernel(L0_attn_x, L0_attn_wq, L0_attn_wk, L0_attn_wv, L0_attn_wo,
           L0_ffn_gate_w, L0_ffn_gate_b, L0_ffn_w1, L0_ffn_b1,
           L0_ffn_w2, L0_ffn_b2, L0_ffn_w3, L0_ffn_b3):
    x = np.ascontiguousarray(np.asarray(L0_attn_x, dtype=np.float32))
    wq = np.asarray(L0_attn_wq, dtype=np.float32)
    wk = np.asarray(L0_attn_wk, dtype=np.float32)
    wv = np.asarray(L0_attn_wv, dtype=np.float32)
    wo = np.asarray(L0_attn_wo, dtype=np.float32)
    gw = np.ascontiguousarray(np.asarray(L0_ffn_gate_w, dtype=np.float32))
    gb = np.asarray(L0_ffn_gate_b, dtype=np.float32).reshape(1, E)
    w123 = [np.asarray(w, dtype=np.float32) for w in (L0_ffn_w1, L0_ffn_w2, L0_ffn_w3)]
    b123 = [np.asarray(b, dtype=np.float32) for b in (L0_ffn_b1, L0_ffn_b2, L0_ffn_b3)]

    pos = np.arange(T, dtype=np.float32)
    dims = np.arange(64, dtype=np.float32)
    freqs = np.power(np.float32(10000.0), -dims / np.float32(64.0)).astype(np.float32)
    rad = (freqs[:, None] * pos[None, :]).astype(np.float32)
    cosv = np.cos(rad).astype(np.float32)
    sinv = np.sin(rad).astype(np.float32)
    cosT = np.ascontiguousarray(np.concatenate([cosv, cosv], axis=0))
    sgnT = np.ascontiguousarray(np.concatenate([-sinv, sinv], axis=0))
    iden = np.eye(128, dtype=np.float32)
    iden8 = np.eye(8, dtype=np.float32)
    swp = np.zeros((128, 128), dtype=np.float32)
    for d in range(128):
        swp[d, (d + 64) % 128] = 1.0
    pj = np.arange(128)[:, None]
    xx = np.arange(1024)[None, :]
    maskT = np.where(xx < pj + 512, np.float32(-1e33), np.float32(0.0)).astype(np.float32)

    in_maps = []
    for c in range(8):
        b, jg = c // 4, c % 4
        sel = np.zeros((8, 128), dtype=np.float32)
        sel[c, :] = 1.0
        xperm = x if b == 0 else np.ascontiguousarray(x[::-1])
        in_maps.append({
            "x": xperm,
            "xres": np.ascontiguousarray(x[b, 2048 * jg:2048 * (jg + 1), :]),
            "wq": np.ascontiguousarray(wq[2 * jg:2 * jg + 2]),
            "wk": np.ascontiguousarray(wk[2 * jg:2 * jg + 2]),
            "wv": np.ascontiguousarray(wv[2 * jg:2 * jg + 2]),
            "wo": np.ascontiguousarray(wo[2 * jg:2 * jg + 2]),
            "gw": gw, "gb": gb,
            "w1": np.ascontiguousarray(w123[0][c]),
            "w2": np.ascontiguousarray(w123[1][c]),
            "w3": np.ascontiguousarray(w123[2][c]),
            "b1": np.ascontiguousarray(b123[0][c].reshape(8, 128)),
            "b2": np.ascontiguousarray(b123[1][c].reshape(8, 128)),
            "b3": np.ascontiguousarray(b123[2][c].reshape(8, 128)),
            "cosT": cosT, "sgnT": sgnT, "iden": iden, "iden8": iden8,
            "swp64": swp, "maskT": maskT, "selrep": sel,
        })

    if not _NC_CACHE:
        _NC_CACHE.append(build_nc())
    nc = _NC_CACHE[0]
    res = run_bass_kernel_spmd(nc, in_maps, core_ids=list(range(8)))
    out = np.empty((B, T, D), dtype=np.float32)
    for c in range(8):
        b, jg = c // 4, c % 4
        out[b, 2048 * jg:2048 * (jg + 1), :] = res.results[c]["out"]
    return out


# version 16
# speedup vs baseline: 1.0047x; 1.0047x over previous
import contextlib
from collections import defaultdict

import numpy as np

import jax

jax.config.update("jax_compilation_cache_dir", "/tmp/jaxcache")
jax.config.update("jax_persistent_cache_min_compile_time_secs", 2.0)

import concourse.bass as bass
import concourse.mybir as mybir
from concourse.bass_utils import run_bass_kernel_spmd

F32 = mybir.dt.float32
BF16 = mybir.dt.bfloat16
AF = mybir.ActivationFunctionType
OP = mybir.AluOpType
AX = mybir.AxisListType

B, T, D, HPC, DH, E = 2, 8192, 1024, 2, 128, 8
NT, TT, MC = 32, 512, 8
NIT = 16
NCH, CH = 8, 2048
NFT = 32
NTOK = B * T
EPS = 1e-5
ENGS = ("sync", "scalar", "vector", "tensor", "gpsimd")

ARENA = 45184


def _carve(cursor, n):
    start = cursor[0]
    cursor[0] += n
    assert cursor[0] <= ARENA, f"arena overflow: {cursor[0]}"
    return start


class Walk:

    def __init__(self, mode, eng, sems):
        self.mode = mode
        self.eng = eng
        self.sems = sems
        self.cnt = defaultdict(int)
        self.snaps = {}

    def do(self, engname, build=None, inc=None, wait=(), n=1):
        if self.mode == engname:
            for w in wait:
                s, v = w if isinstance(w, tuple) else self.snaps[w]
                if v > 0:
                    self.eng.wait_ge(self.sems[s], v)
            inst = build(self.eng) if build is not None else None
        else:
            inst = None
        if inc is not None:
            self.cnt[inc] += n
            if inst is not None:
                if isinstance(inst, list):
                    assert n == 16 * len(inst), (n, len(inst))
                    for i in inst:
                        i.then_inc(self.sems[inc], 16)
                else:
                    inst.then_inc(self.sems[inc], n)
        return inst

    def snap(self, key, sem):
        self.snaps[key] = (sem, self.cnt[sem])


def build_nc():
    nc = bass.Bass("TRN2", target_bir_lowering=False)

    xin = nc.dram_tensor("x", [B, T, D], F32, kind="ExternalInput").ap()
    xres_in = nc.dram_tensor("xres", [2048, D], F32, kind="ExternalInput").ap()
    wq_in = nc.dram_tensor("wq", [HPC, D, DH], F32, kind="ExternalInput").ap()
    wk_in = nc.dram_tensor("wk", [HPC, D, DH], F32, kind="ExternalInput").ap()
    wv_in = nc.dram_tensor("wv", [HPC, D, DH], F32, kind="ExternalInput").ap()
    wo_in = nc.dram_tensor("wo", [HPC, DH, D], F32, kind="ExternalInput").ap()
    gw_in = nc.dram_tensor("gw", [D, E], F32, kind="ExternalInput").ap()
    gb_in = nc.dram_tensor("gb", [1, E], F32, kind="ExternalInput").ap()
    w_in = [nc.dram_tensor(f"w{i}", [D, D], F32, kind="ExternalInput").ap() for i in (1, 2, 3)]
    b_in = [nc.dram_tensor(f"b{i}", [8, 128], F32, kind="ExternalInput").ap() for i in (1, 2, 3)]
    cos_in = nc.dram_tensor("cosT", [128, T], F32, kind="ExternalInput").ap()
    sgn_in = nc.dram_tensor("sgnT", [128, T], F32, kind="ExternalInput").ap()
    iden_in = nc.dram_tensor("iden", [128, 128], F32, kind="ExternalInput").ap()
    iden8_in = nc.dram_tensor("iden8", [8, 8], F32, kind="ExternalInput").ap()
    swap_in = nc.dram_tensor("swp64", [128, 128], F32, kind="ExternalInput").ap()
    mask_in = nc.dram_tensor("maskT", [128, 1024], F32, kind="ExternalInput").ap()
    sel_in = nc.dram_tensor("selrep", [8, 128], F32, kind="ExternalInput").ap()
    out_t = nc.dram_tensor("out", [2048, D], F32, kind="ExternalOutput").ap()

    xnscr = nc.dram_tensor("xnscr", [MC, 128, NTOK], F32).ap()
    kscr = nc.dram_tensor("kscr", [HPC, 128, T], F32).ap()
    qscr = nc.dram_tensor("qscr", [HPC, 128, T], F32).ap()
    vscr = nc.dram_tensor("vscr", [HPC, T, DH], F32).ap()
    lgtscr = nc.dram_tensor("lgtscr", [8, NTOK], F32).ap()
    wtscr = nc.dram_tensor("wtscr", [8, NTOK], F32).ap()
    rsin_t = nc.dram_tensor("rsin", [T, 1040], F32)
    rsout_t = nc.dram_tensor("rsout", [2048, 1040], F32)
    agvin_t = nc.dram_tensor("agvin", [8, 128], F32)
    agvout_t = nc.dram_tensor("agvout", [64, 128], F32)
    rsin, rsout = rsin_t.ap(), rsout_t.ap()
    agvout = agvout_t.ap()
    agvin = agvin_t.ap()

    cur = [0]
    XB = [_carve(cur, 4096), _carve(cur, 4096)]
    XF = [_carve(cur, 4096), _carve(cur, 4096)]
    WQ, WK, WV = _carve(cur, 2048), _carve(cur, 2048), _carve(cur, 2048)
    GW = _carve(cur, 64)
    TC = [_carve(cur, 512), _carve(cur, 512)]
    TS = [_carve(cur, 512), _carve(cur, 512)]
    RAW = [_carve(cur, 512), _carve(cur, 512)]
    TMA = [_carve(cur, 512), _carve(cur, 512)]
    TMB = [_carve(cur, 512), _carve(cur, 512)]
    VS1 = [_carve(cur, 512), _carve(cur, 512)]
    VS2 = [_carve(cur, 512), _carve(cur, 512)]
    LGS = [_carve(cur, 512), _carve(cur, 512)]
    STAT = _carve(cur, 64)
    SCR = _carve(cur, 1024)
    assert cur[0] <= 44544, cur[0]
    W1, W2, W3 = 0, 8192, 16384
    XNF = [24576, 28672]
    H1, H2 = 32768, 36864
    YB = [40960, 41472]
    WY = 41984
    WTSL = [42496, 43008]
    VPART = 43520
    assert VPART + 264 <= 44544
    LGB = [24576, 26624]
    EGB = 28672
    PRB = 30720
    PTM = 32768
    TK1, TK2 = 32896, 32912
    OH1, PM2, TMG, WTM, INDQ, W1T, W2T = 33024, 33152, 33280, 33408, 33536, 33664, 33792
    WTS = [33920, 34432]
    GACC = 34944
    GSUM = 34960
    LGBIG = 24576
    EGB2 = 36352
    PRB2 = 38400
    PTM2 = 40448
    INDPAD = 44000
    WO2 = 24576
    QB = [26624, 27136]
    KB = [27648, 28160]
    VB = [28672, 29184]
    EB = [29696, 30208, 30720]
    ONM = [31232, 31744]
    IZR = 32256
    IZV = 32768
    RPB = [33280, 33792]
    MSK = 35328
    assert MSK + 1024 <= 44544
    OUTB = [33088, 34112]
    VALL = 35136
    INDE = [36160, 36288]
    PERS = 44544
    IDEN = PERS
    IDEN8 = PERS + 128
    SELR = PERS + 136
    GB = PERS + 264
    B123 = PERS + 265
    SWP = PERS + 289
    assert SWP + 128 <= ARENA

    own = lambda t: t < NT // 2

    with contextlib.ExitStack() as es:
        ar_t = es.enter_context(nc.sbuf_tensor("arena", [128, ARENA], F32))
        ebf_t = [es.enter_context(nc.sbuf_tensor(f"ebf{i}", [128, 512], BF16)) for i in range(2)]
        onesb_t = es.enter_context(nc.sbuf_tensor("onesb", [128, 1], BF16))
        ones1_t = es.enter_context(nc.sbuf_tensor("ones1", [1, 128], F32))
        ps_t = [es.enter_context(nc.psum_tensor(f"ps{i}", [128, 512], F32)) for i in range(8)]
        sems = {}
        for s in ("s_sy", "s_gp", "s_pe", "s_act", "s_dve", "s_cc", "s_bar"):
            sems[s] = es.enter_context(nc.semaphore(s))
        block = es.enter_context(nc.Block())

        ar = ar_t.ap()
        ebf = [t.ap() for t in ebf_t]
        onesb = onesb_t.ap()
        ones1 = ones1_t.ap()
        ps = [t.ap() for t in ps_t]

        def A(col, w, p0=0, pn=128):
            return ar[p0:p0 + pn, col:col + w]

        def program(P: Walk):
            nbar = [0]

            def barrier():
                nbar[0] += 1
                sy_c, gp_c = P.cnt["s_sy"], P.cnt["s_gp"]
                for e in ENGS:
                    P.do(e, lambda g: g.nop(), inc="s_bar")
                for e in ENGS:
                    P.do(e, wait=[("s_bar", 5 * nbar[0]), ("s_sy", sy_c), ("s_gp", gp_c)])

            def ld(g, dst, src):
                return g.dma_start(out=dst, in_=src)

            P.do("sync", lambda g: [ld(g, A(IDEN, 128), iden_in[:, :]),
                                    ld(g, A(IDEN8, 8, 0, 8), iden8_in[:, :]),
                                    ld(g, A(SELR, 128, 0, 8), sel_in[:, :]),
                                    ld(g, A(SWP, 128), swap_in[:, :]),
                                    ld(g, A(MSK, 1024), mask_in[:, :])],
                 inc="s_sy", n=16 * 5)
            P.do("gpsimd", lambda g: [
                g.dma_start(out=A(GB, 1, 0, 8), in_=gb_in.transpose((1, 0)),
                            allow_slow_non_contiguous=True)] + [
                g.dma_start(out=A(B123 + 8 * i, 8), in_=b_in[i].transpose((1, 0)),
                            allow_slow_non_contiguous=True) for i in range(3)],
                 inc="s_gp", n=16 * 4)
            P.do("sync", lambda g: [
                ld(g, A(Wc + (h * 8 + mc) * 128, 128), w_ap[h, mc * 128:(mc + 1) * 128, :])
                for h in range(HPC) for mc in range(MC)
                for Wc, w_ap in ((WQ, wq_in), (WK, wk_in), (WV, wv_in))],
                 inc="s_sy", n=16 * 48)
            P.do("sync", lambda g: [ld(g, A(GW + 8 * mc, 8), gw_in[mc * 128:(mc + 1) * 128, :])
                                    for mc in range(MC)], inc="s_sy", n=16 * 8)
            P.do("vector", lambda v: v.memset(onesb[:, :], 1.0))
            P.do("vector", lambda v: v.memset(A(INDPAD, 256), 0.0))
            P.do("vector", lambda v: v.memset(ones1[:, :], 1.0))
            P.snap("consts", "s_sy")
            P.snap("consts_gp", "s_gp")

            aux_i = [0]
            aux_free = {}
            prj_i = [0]
            prj_free = {}

            def aux_bank():
                i = aux_i[0]
                aux_i[0] += 1
                w = [aux_free[i - 2]] if (i - 2) in aux_free else []
                return i, ps[i % 2], w

            def prj_bank():
                i = prj_i[0]
                prj_i[0] += 1
                w = [prj_free[i - 2]] if (i - 2) in prj_free else []
                return i, ps[3 + i % 2], w

            for t in range(NT):
                bt, p0 = t // (NT // 2), TT * (t % (NT // 2))
                gt0 = TT * t
                bf = t % 2
                sb = 32 * bf
                wl = [f"xbdone_{t-2}"] if t >= 2 else []
                P.do("sync", lambda g, bt=bt, p0=p0, bf=bf: [
                    ld(g, A(XB[bf] + 1024 * k, 1024), xin[bt, p0 + 128 * k: p0 + 128 * (k + 1), :])
                    for k in range(4)], inc="s_sy", n=64, wait=wl)
                P.snap(f"xload_{t}", "s_sy")
                if own(t):
                    wt = [f"ropedone_{t-2}"] if t >= 2 else []
                    P.do("sync", lambda g, p0=p0, bf=bf: [
                        ld(g, A(TC[bf], 512), cos_in[:, p0:p0 + 512]),
                        ld(g, A(TS[bf], 512), sgn_in[:, p0:p0 + 512])],
                        inc="s_sy", n=32, wait=wt)
                    P.snap(f"tload_{t}", "s_sy")
                P.do("scalar", lambda s, bf=bf, sb=sb: [
                    (s.activation(A(SCR, 1024), A(XB[bf] + 1024 * k, 1024), AF.Copy,
                                  accum_out=A(STAT + sb + k, 1)),
                     s.activation(A(SCR, 1024), A(XB[bf] + 1024 * k, 1024), AF.Square,
                                  accum_out=A(STAT + sb + 4 + k, 1)))
                    for k in range(4)][-1][-1], inc="s_act", wait=[f"xload_{t}"])
                P.snap(f"stats_{t}", "s_act")
                P.do("vector", lambda v, sb=sb: [
                    v.tensor_scalar(A(STAT + sb + 8, 4), A(STAT + sb, 4), 1.0 / D, None, OP.mult),
                    v.tensor_scalar(A(STAT + sb + 12, 4), A(STAT + sb + 4, 4), 1.0 / D, EPS, OP.mult, OP.add),
                ][-1], inc="s_dve", wait=[f"stats_{t}"])
                P.snap(f"lnstat0_{t}", "s_dve")
                P.do("vector", lambda v, sb=sb:
                     v.tensor_tensor(A(STAT + sb + 16, 4), A(STAT + sb + 8, 4), A(STAT + sb + 8, 4), OP.mult),
                     inc="s_dve", wait=[f"lnstat0_{t}"])
                P.snap(f"lnstat1_{t}", "s_dve")
                P.do("vector", lambda v, sb=sb:
                     v.tensor_tensor(A(STAT + sb + 20, 4), A(STAT + sb + 12, 4), A(STAT + sb + 16, 4), OP.subtract),
                     inc="s_dve", wait=[f"lnstat1_{t}"])
                P.snap(f"lnstat_{t}", "s_dve")
                P.do("scalar", lambda s, sb=sb: s.activation(A(STAT + sb + 24, 4), A(STAT + sb + 20, 4), AF.Sqrt),
                     inc="s_act", wait=[f"lnstat_{t}"])
                P.snap(f"sden_{t}", "s_act")
                P.do("vector", lambda v, sb=sb: v.reciprocal(A(STAT + sb + 28, 4), A(STAT + sb + 24, 4)),
                     inc="s_dve", wait=[f"sden_{t}"])
                P.snap(f"rstd0_{t}", "s_dve")
                P.do("vector", lambda v, sb=sb:
                     v.tensor_tensor(A(STAT + sb + 16, 4), A(STAT + sb + 28, 4), A(STAT + sb + 28, 4), OP.mult),
                     inc="s_dve", wait=[f"rstd0_{t}"])
                P.snap(f"nt1_{t}", "s_dve")
                P.do("vector", lambda v, sb=sb:
                     v.tensor_tensor(A(STAT + sb + 16, 4), A(STAT + sb + 20, 4), A(STAT + sb + 16, 4), OP.mult),
                     inc="s_dve", wait=[f"nt1_{t}"])
                P.snap(f"nt2_{t}", "s_dve")
                P.do("vector", lambda v, sb=sb:
                     v.tensor_scalar(A(STAT + sb + 16, 4), A(STAT + sb + 16, 4), -0.5, 1.5, OP.mult, OP.add),
                     inc="s_dve", wait=[f"nt2_{t}"])
                P.snap(f"nt3_{t}", "s_dve")
                P.do("vector", lambda v, sb=sb:
                     v.tensor_tensor(A(STAT + sb + 28, 4), A(STAT + sb + 28, 4), A(STAT + sb + 16, 4), OP.mult),
                     inc="s_dve", wait=[f"nt3_{t}"])
                P.snap(f"rstd_{t}", "s_dve")
                P.do("vector", lambda v, sb=sb, bf=bf: [
                    v.tensor_scalar(A(XB[bf] + 1024 * k, 1024), A(XB[bf] + 1024 * k, 1024),
                                    A(STAT + sb + 8 + k, 1), A(STAT + sb + 28 + k, 1),
                                    OP.subtract, OP.mult) for k in range(4)][-1],
                     inc="s_dve", wait=[f"rstd_{t}"])
                P.snap(f"xn_{t}", "s_dve")
                for mc in range(MC):
                    ai, bank, wfree = aux_bank()
                    P.do("tensor", lambda te, bf=bf, mc=mc, bank=bank: [
                        te.transpose(bank[:, 128 * k:128 * (k + 1)],
                                     A(XB[bf] + 1024 * k + 128 * mc, 128), A(IDEN, 128))
                        for k in range(4)][-1], inc="s_pe",
                         wait=[f"xn_{t}", "consts"] + wfree)
                    P.snap(f"tr_{t}_{mc}", "s_pe")
                    wx = ([f"xfdone_{t-2}", f"xfpe_{t-2}"] if (t >= 2 and mc == 0) else [])
                    P.do("scalar", lambda s, bf=bf, mc=mc, bank=bank:
                         s.activation(A(XF[bf] + 512 * mc, 512), bank[:, 0:512], AF.Copy),
                         inc="s_act", wait=[f"tr_{t}_{mc}"] + wx)
                    P.snap(f"xfev_{t}_{mc}", "s_act")
                    aux_free[ai] = f"xfev_{t}_{mc}"
                for mc in range(MC):
                    wg = [f"lgev_{t-1}"] if (t >= 1 and mc == 0) else []
                    P.do("tensor", lambda te, bf=bf, mc=mc:
                         te.matmul(ps[2][0:8, 0:512], A(GW + 8 * mc, 8), A(XF[bf] + 512 * mc, 512),
                                   start=(mc == 0), stop=(mc == MC - 1)),
                         inc="s_pe" if mc == MC - 1 else None,
                         wait=[f"xfev_{t}_{mc}"] + wg)
                P.snap(f"gmm_{t}", "s_pe")
                wlg = [f"lgw_{t-2}"] if t >= 2 else []
                P.do("scalar", lambda s, bf=bf: s.activation(A(LGS[bf], 512, 0, 8), ps[2][0:8, 0:512],
                                                             AF.Identity, bias=A(GB, 1, 0, 8)),
                     inc="s_act", wait=[f"gmm_{t}", "consts_gp"] + wlg)
                P.snap(f"lgev_{t}", "s_act")
                P.do("gpsimd", lambda g, bf=bf, gt0=gt0: [g.dma_start(out=lgtscr[:, gt0:gt0 + 512],
                                                                      in_=A(LGS[bf], 512, 0, 8))],
                     inc="s_gp", n=16, wait=[f"lgev_{t}"])
                P.snap(f"lgw_{t}", "s_gp")
                if own(t):
                    for h in range(HPC):
                        for X, Wc in (("k", WK), ("v", WV), ("q", WQ)):
                            pi, pbank, wfree = prj_bank()
                            for mc in range(MC):
                                P.do("tensor", lambda te, h=h, mc=mc, Wc=Wc, pbank=pbank, bf=bf:
                                     te.matmul(pbank[:, 0:512], A(Wc + (h * 8 + mc) * 128, 128),
                                               A(XF[bf] + 512 * mc, 512),
                                               start=(mc == 0), stop=(mc == MC - 1)),
                                     inc="s_pe" if mc == MC - 1 else None,
                                     wait=[f"xfev_{t}_{mc}"] + (wfree if mc == 0 else []))
                            P.snap(f"pmm_{t}_{h}_{X}", "s_pe")
                            if X == "v":
                                wv2 = [f"vw_{t-2}_{h}"] if t >= 2 else []
                                P.do("scalar", lambda s, bf=bf, pbank=pbank:
                                     s.activation(A(VS1[bf], 512), pbank[:, 0:512], AF.Copy),
                                     inc="s_act", wait=[f"pmm_{t}_{h}_{X}"] + wv2)
                                P.snap(f"vev1_{t}_{h}", "s_act")
                                prj_free[pi] = f"vev1_{t}_{h}"
                                ai, bank, wfree2 = aux_bank()
                                P.do("tensor", lambda te, bf=bf, bank=bank: [
                                    te.transpose(bank[:, 128 * k:128 * (k + 1)],
                                                 A(VS1[bf] + 128 * k, 128), A(IDEN, 128))
                                    for k in range(4)][-1], inc="s_pe",
                                     wait=[f"vev1_{t}_{h}"] + wfree2)
                                P.snap(f"vtr_{t}_{h}", "s_pe")
                                P.do("scalar", lambda s, bf=bf, bank=bank:
                                     s.activation(A(VS2[bf], 512), bank[:, 0:512], AF.Copy),
                                     inc="s_act", wait=[f"vtr_{t}_{h}"])
                                P.snap(f"vev2_{t}_{h}", "s_act")
                                aux_free[ai] = f"vev2_{t}_{h}"
                                P.do("gpsimd", lambda g, bf=bf, h=h, p0=p0: [
                                    g.dma_start(out=vscr[h, p0 + 128 * k:p0 + 128 * (k + 1), :],
                                                in_=A(VS2[bf] + 128 * k, 128))
                                    for k in range(4)], inc="s_gp", n=64,
                                     wait=[f"vev2_{t}_{h}"])
                                P.snap(f"vw_{t}_{h}", "s_gp")
                            else:
                                wr2 = [f"{X}w_{t-2}_{h}"] if t >= 2 else []
                                P.do("scalar", lambda s, bf=bf, pbank=pbank:
                                     s.activation(A(RAW[bf], 512), pbank[:, 0:512], AF.Copy),
                                     inc="s_act", wait=[f"pmm_{t}_{h}_{X}"] + wr2)
                                P.snap(f"rawev_{t}_{h}_{X}", "s_act")
                                prj_free[pi] = f"rawev_{t}_{h}_{X}"
                                ai, bank, wfree2 = aux_bank()
                                P.do("tensor", lambda te, bf=bf, bank=bank:
                                     te.matmul(bank[:, 0:512], A(SWP, 128), A(RAW[bf], 512),
                                               start=True, stop=True),
                                     inc="s_pe", wait=[f"rawev_{t}_{h}_{X}"] + wfree2)
                                P.snap(f"swp_{t}_{h}_{X}", "s_pe")
                                P.do("vector", lambda v, bf=bf, bank=bank: [
                                    v.tensor_tensor(A(TMA[bf], 512), A(RAW[bf], 512), A(TC[bf], 512), OP.mult),
                                    v.tensor_tensor(A(TMB[bf], 512), bank[:, 0:512], A(TS[bf], 512), OP.mult),
                                ][-1], inc="s_dve",
                                     wait=[f"swp_{t}_{h}_{X}", f"tload_{t}"])
                                P.snap(f"ropem_{t}_{h}_{X}", "s_dve")
                                P.do("vector", lambda v, bf=bf:
                                     v.tensor_tensor(A(RAW[bf], 512), A(TMA[bf], 512), A(TMB[bf], 512), OP.add),
                                     inc="s_dve", wait=[f"ropem_{t}_{h}_{X}"])
                                P.snap(f"rope_{t}_{h}_{X}", "s_dve")
                                aux_free[ai] = f"rope_{t}_{h}_{X}"
                                scr_ap = kscr if X == "k" else qscr
                                P.do("gpsimd", lambda g, bf=bf, h=h, p0=p0, scr_ap=scr_ap:
                                     [g.dma_start(out=scr_ap[h, :, p0:p0 + 512], in_=A(RAW[bf], 512))],
                                     inc="s_gp", n=16, wait=[f"rope_{t}_{h}_{X}"])
                                P.snap(f"{X}w_{t}_{h}", "s_gp")
                    P.snap(f"ropedone_{t}", "s_dve")
                P.do("gpsimd", lambda g, bf=bf, gt0=gt0: [
                    g.dma_start(out=xnscr[mc][:, gt0:gt0 + 512], in_=A(XF[bf] + 512 * mc, 512))
                    for mc in range(MC)], inc="s_gp", n=16 * MC,
                     wait=[f"xfev_{t}_{MC-1}"])
                P.snap(f"xnw_{t}", "s_gp")
                P.snap(f"xbdone_{t}", "s_pe")
                P.snap(f"xfdone_{t}", "s_gp")
                P.snap(f"xfpe_{t}", "s_pe")

            barrier()

            for bt in range(2):
                P.do("sync", lambda g, bt=bt: [ld(g, A(LGBIG, 8192, 0, 8),
                                                  lgtscr[:, 8192 * bt:8192 * (bt + 1)])],
                     inc="s_sy", n=16, wait=[f"gexp1_{bt-1}"] if bt >= 1 else [])
                P.snap(f"glgl_{bt}", "s_sy")
                P.do("scalar", lambda s, bt=bt:
                     s.activation(A(LGBIG, 8192, 32, 8), A(LGBIG, 8192, 0, 8),
                                  AF.Exp, accum_out=A(GACC + bt, 1, 0, 8)),
                     inc="s_act", wait=[f"glgl_{bt}"])
                P.snap(f"gexp1_{bt}", "s_act")
            P.do("vector", lambda v: v.reciprocal(A(GSUM + 2, 2, 0, 8), A(GACC, 2, 0, 8)),
                 inc="s_dve", wait=[f"gexp1_{1}"])
            P.snap("ginv", "s_dve")
            for ch in range(NCH):
                bt = ch // 4
                egb_c = EGB if ch % 2 == 0 else EGB2
                prb_c = PRB if ch % 2 == 0 else PRB2
                ptm_c = PTM if ch % 2 == 0 else PTM2
                P.do("sync", lambda g, ch=ch: [ld(g, A(LGB[ch % 2], 2048, 0, 8),
                                                  lgtscr[:, CH * ch:CH * (ch + 1)])],
                     inc="s_sy", n=16, wait=[f"gexp2_{ch-2}"] if ch >= 2 else ["ginv"])
                P.snap(f"glg2_{ch}", "s_sy")
                P.do("scalar", lambda s, ch=ch, egb_c=egb_c: s.activation(
                        A(egb_c, 2048, 0, 8), A(LGB[ch % 2], 2048, 0, 8), AF.Exp),
                     inc="s_act", wait=[f"glg2_{ch}"] + ([f"gtr_{ch-2}"] if ch >= 2 else []))
                P.snap(f"gexp2_{ch}", "s_act")
                P.do("vector", lambda v, bt=bt, egb_c=egb_c, prb_c=prb_c: v.tensor_scalar(
                        A(prb_c, 2048, 0, 8), A(egb_c, 2048, 0, 8),
                        A(GSUM + 2 + bt, 1, 0, 8), None, OP.mult),
                     inc="s_dve", wait=[f"gexp2_{ch}", "ginv"] + ([f"gtr_{ch-2}"] if ch >= 2 else []))
                P.snap(f"gprb_{ch}", "s_dve")
                P.do("tensor", lambda te, ch=ch, prb_c=prb_c: [
                    te.matmul(ps[ch % 2][:, 8 * blk:8 * (blk + 1)], A(prb_c + 128 * blk, 128, 0, 8),
                              A(IDEN8, 8, 0, 8), start=True, stop=True)
                    for blk in range(16)][-1], inc="s_pe",
                     wait=[f"gprb_{ch}"] + ([f"gtk_{ch-2}"] if ch >= 2 else []))
                P.snap(f"gtr_{ch}", "s_pe")
                P.do("scalar", lambda s, ch=ch, ptm_c=ptm_c: s.activation(
                        A(ptm_c, 128), ps[ch % 2][:, 0:128], AF.Copy),
                     inc="s_act", wait=[f"gtr_{ch}"] + ([f"gtk_{ch-2}"] if ch >= 2 else []))
                P.snap(f"gptm_{ch}", "s_act")
                p3 = A(ptm_c, 128).rearrange("p (g e) -> p g e", e=8)

                t1b = A(TK1, 16).unsqueeze(-1).broadcast_to((128, 16, 8))
                t2b = A(TK2, 16).unsqueeze(-1).broadcast_to((128, 16, 8))
                o1_3 = A(OH1, 128).rearrange("p (g e) -> p g e", e=8)
                o2_3 = A(PM2, 128).rearrange("p (g e) -> p g e", e=8)
                pm3 = A(TMG, 128).rearrange("p (g e) -> p g e", e=8)
                wtk = [f"gptm_{ch}"]
                if 0 < ch <= 4:
                    wtk.append(f"gindw_{ch-1}")
                steps = [
                    lambda v: v.reduce_max(A(TK1, 16), p3, axis=AX.X),
                    lambda v: v.tensor_tensor(o1_3, p3, t1b, OP.is_equal),
                    lambda v: v.tensor_scalar(A(W1T, 128), A(OH1, 128), 1e30, None, OP.mult),
                    lambda v, ptm_c=ptm_c: v.tensor_tensor(A(TMG, 128), A(ptm_c, 128), A(W1T, 128), OP.subtract),
                    lambda v: v.reduce_max(A(TK2, 16), pm3, axis=AX.X),
                    lambda v: v.tensor_tensor(o2_3, pm3, t2b, OP.is_equal),
                    lambda v: v.tensor_tensor(A(W1T, 128).rearrange("p (g e) -> p g e", e=8), o1_3, t1b, OP.mult),
                    lambda v: v.tensor_tensor(A(W2T, 128).rearrange("p (g e) -> p g e", e=8), o2_3, t2b, OP.mult),
                    lambda v: v.tensor_tensor(A(WTM, 128), A(W1T, 128), A(W2T, 128), OP.add),
                    lambda v: v.tensor_tensor(A(INDQ, 128), A(OH1, 128), A(PM2, 128), OP.add),
                    lambda v: v.tensor_scalar(A(INDQ, 128), A(INDQ, 128), 0.25, None, OP.mult),
                ]
                for si, st_fn in enumerate(steps):
                    P.do("vector", st_fn, inc="s_dve",
                         wait=(wtk if si == 0 else [f"gtk{si-1}_{ch}"]))
                    P.snap(f"gtk{si}_{ch}", "s_dve")
                P.snap(f"gtk_{ch}", "s_dve")
                if ch < 4:
                    P.do("vector", lambda v: [
                        v.tensor_copy(A(INDPAD + 16 * blk, 8), A(INDQ + 8 * blk, 8))
                        for blk in range(16)][-1], inc="s_dve",
                         wait=[f"gtk_{ch}"] + ([f"gindw_{ch-1}"] if ch >= 1 else []))
                    P.snap(f"gindp_{ch}", "s_dve")
                for g4 in range(4):
                    P.do("tensor", lambda te, g4=g4: [
                        te.matmul(ps[2 + g4 % 2][0:8, 128 * q:128 * (q + 1)],
                                  A(WTM + 8 * (4 * g4 + q), 8), A(IDEN, 128), start=True, stop=True)
                        for q in range(4)][-1], inc="s_pe",
                         wait=[f"gtk_{ch}"] + ([f"gwev_{ch}_{g4-2}"] if g4 >= 2
                                               else ([f"gwev_{ch-1}_{g4+2}"] if ch >= 1 else [])))
                    P.snap(f"gwtr_{ch}_{g4}", "s_pe")
                    P.do("scalar", lambda s, g4=g4: s.activation(A(WTS[g4 % 2], 512, 0, 8),
                                                                 ps[2 + g4 % 2][0:8, 0:512], AF.Copy),
                         inc="s_act", wait=[f"gwtr_{ch}_{g4}"] + ([f"gww_{ch}_{g4-2}"] if g4 >= 2
                                                                  else ([f"gww_{ch-1}_{g4+2}"] if ch >= 1 else [])))
                    P.snap(f"gwev_{ch}_{g4}", "s_act")
                    P.do("gpsimd", lambda g, ch=ch, g4=g4: [g.dma_start(
                        out=wtscr[:, CH * ch + 512 * g4: CH * ch + 512 * (g4 + 1)],
                        in_=A(WTS[g4 % 2], 512, 0, 8))], inc="s_gp", n=16, wait=[f"gwev_{ch}_{g4}"])
                    P.snap(f"gww_{ch}_{g4}", "s_gp")
                if ch < 4:
                    P.do("gpsimd", lambda g, ch=ch: [
                        g.dma_start(out=rsin[CH * ch + 128 * blk: CH * ch + 128 * (blk + 1), 1024:1040],
                                    in_=A(INDPAD + 16 * blk, 16))
                        for blk in range(16)], inc="s_gp", n=16 * 16, wait=[f"gindp_{ch}"])
                    P.snap(f"gindw_{ch}", "s_gp")

            barrier()

            P.do("sync", lambda g: [
                ld(g, A(Wc + 1024 * mc, 1024), w_ap[128 * mc:128 * (mc + 1), :])
                for Wc, w_ap in ((W1, w_in[0]), (W2, w_in[1]), (W3, w_in[2]))
                for mc in range(MC)] + [
                ld(g, A(WO2 + 1024 * h, 1024), wo_in[h, :, :]) for h in range(HPC)],
                 inc="s_sy", n=16 * 26)
            P.snap("wffn", "s_sy")

            for it in range(NIT):
                i0 = 512 * it
                for h in range(HPC):
                    a = 2 * it + h
                    P.do("sync", lambda g, h=h, i0=i0, a=a: [ld(g, A(QB[a % 2], 512), qscr[h, :, i0:i0 + 512])],
                         inc="s_sy", n=16, wait=[f"apvlast_{a-2}"] if a >= 2 else [])
                    P.snap(f"aql_{it}_{h}", "s_sy")
                    units = [(jc, js) for jc in range(it + 1) for js in range(4)]
                    nu = len(units)

                    def kvload(jc, wv):
                        P.do("sync", lambda g, h=h, jc=jc: [
                            ld(g, A(KB[jc % 2], 512), kscr[h, :, 512 * jc:512 * (jc + 1)]),
                            ld(g, A(VB[jc % 2], 512).rearrange("p (s d) -> p s d", s=4),
                               vscr[h, 512 * jc:512 * (jc + 1), :].rearrange("(s p) d -> p s d", p=128)),
                        ], inc="s_sy", n=32, wait=wv)
                        P.snap(f"akvl_{it}_{h}_{jc}", "s_sy")

                    for jc in range(min(2, it + 1)):
                        kvload(jc, [f"apvlast_{a-1}"] if a >= 1 else [])
                    for u in range(nu + 1):
                        if u < nu:
                            jc, js = units[u]
                            diag = (jc == it)
                            wS = [f"akvl_{it}_{h}_{jc}", f"aql_{it}_{h}"]
                            if u >= 2:
                                wS.append(f"aexp_{it}_{h}_{u-2}")
                            elif a >= 1:
                                wS.append(f"aexpl_{a-1}")
                            P.do("tensor", lambda te, jc=jc, js=js, u=u, a=a:
                                 te.matmul(ps[u % 2][:, 0:512], A(KB[jc % 2] + 128 * js, 128),
                                           A(QB[a % 2], 512), start=True, stop=True),
                                 inc="s_pe", wait=wS)
                            P.snap(f"asmm_{it}_{h}_{u}", "s_pe")
                            if diag:
                                d = 128 * js
                                P.do("vector", lambda v, u=u, d=d:
                                     v.tensor_tensor(ps[u % 2][:, 0:512], ps[u % 2][:, 0:512],
                                                     A(MSK + 512 - d, 512), OP.add),
                                     inc="s_dve", wait=[f"asmm_{it}_{h}_{u}"])
                                P.snap(f"amsk_{it}_{h}_{u}", "s_dve")
                        if u > 0:
                            up = u - 1
                            jcp, jsp = units[up]
                            diagp = (jcp == it)
                            wE = [f"amsk_{it}_{h}_{up}" if diagp else f"asmm_{it}_{h}_{up}"]
                            if up >= 3:
                                wE.append(f"apv_{it}_{h}_{up-3}")
                            P.do("scalar", lambda s, up=up: s.activation(A(EB[up % 3], 512),
                                                                         ps[up % 2][:, 0:512],
                                                                         AF.Exp, scale=1.0 / 128.0),
                                 inc="s_act", wait=wE)
                            P.snap(f"aexp_{it}_{h}_{up}", "s_act")
                            if up == nu - 1:
                                P.snap(f"aexpl_{a}", "s_act")
                            wC = [f"aexp_{it}_{h}_{up}"]
                            if up >= 2:
                                wC.append(f"az_{it}_{h}_{up-2}")
                            P.do("vector", lambda v, up=up: v.tensor_copy(ebf[up % 2][:, :], A(EB[up % 3], 512)),
                                 inc="s_dve", wait=wC)
                            P.snap(f"acst_{it}_{h}_{up}", "s_dve")
                            wP = [f"aexp_{it}_{h}_{up}"]
                            if up == 0 and a >= 2:
                                wP.append(f"aofree_{a-2}")
                            P.do("tensor", lambda te, jcp=jcp, jsp=jsp, up=up, a=a:
                                 te.matmul(ps[2 + a % 2][:, 0:512], A(VB[jcp % 2] + 128 * jsp, 128),
                                           A(EB[up % 3], 512), start=(up == 0), stop=(up == nu - 1)),
                                 inc="s_pe", wait=wP)
                            P.snap(f"apv_{it}_{h}_{up}", "s_pe")
                            wZ = [f"acst_{it}_{h}_{up}"]
                            if up == 0 and a >= 2:
                                wZ.append(f"azfree_{a-2}")
                            P.do("tensor", lambda te, up=up, a=a:
                                 te.matmul(ps[4 + a % 2][0:1, 0:512], onesb[:, :], ebf[up % 2][:, :],
                                           start=(up == 0), stop=(up == nu - 1)),
                                 inc="s_pe", wait=wZ)
                            P.snap(f"az_{it}_{h}_{up}", "s_pe")
                            if u % 4 == 0 and u > 0:
                                jcl = u // 4 + 1
                                if jcl <= it:
                                    kvload(jcl, [f"apv_{it}_{h}_{4*jcl-5}"])
                    P.snap(f"apvlast_{a}", "s_pe")
                    P.do("vector", lambda v, a=a: v.reciprocal(A(IZV, 512, 0, 1), ps[4 + a % 2][0:1, 0:512]),
                         inc="s_dve", wait=[f"az_{it}_{h}_{nu-1}"])
                    P.snap(f"aivz_{it}_{h}", "s_dve")
                    P.snap(f"azfree_{a}", "s_dve")
                    wrep = [f"aivz_{it}_{h}"]
                    if h == 1:
                        wrep.append(f"aonm_{it}_0")
                    elif it >= 1:
                        wrep.append(f"aonm_{it-1}_1")
                        wrep.append(f"arev_{it-1}_6")
                    P.do("tensor", lambda te: te.matmul(ps[6][:, 0:512], ones1[:, :], A(IZV, 512, 0, 1),
                                                        start=True, stop=True),
                         inc="s_pe", wait=wrep)
                    P.snap(f"arep_{it}_{h}", "s_pe")
                    P.do("vector", lambda v: v.tensor_copy(A(IZR, 512), ps[6][:, 0:512]),
                         inc="s_dve", wait=[f"arep_{it}_{h}"])
                    P.snap(f"aizr_{it}_{h}", "s_dve")
                    P.do("vector", lambda v, h=h, a=a:
                         v.tensor_tensor(A(ONM[h], 512), ps[2 + a % 2][:, 0:512], A(IZR, 512), OP.mult),
                         inc="s_dve", wait=[f"aizr_{it}_{h}", f"apv_{it}_{h}_{nu-1}"])
                    P.snap(f"aonm_{it}_{h}", "s_dve")
                    P.snap(f"aofree_{a}", "s_dve")
                r_i = 0
                for ib in range(4):
                    for mh in range(2):
                        wR = [f"aonm_{it}_1", "wffn"]
                        if r_i >= 2:
                            wR.append(f"arev_{it}_{r_i-2}")
                        elif it >= 1:
                            wR.append(f"arev_{it-1}_{r_i+6}")
                        P.do("tensor", lambda te, ib=ib, mh=mh, r_i=r_i: [
                            te.matmul(ps[6 + r_i % 2][:, 0:512], A(ONM[h2] + 128 * ib, 128),
                                      A(WO2 + 1024 * h2 + 512 * mh, 512),
                                      start=(h2 == 0), stop=(h2 == 1))
                            for h2 in range(HPC)][-1], inc="s_pe", wait=wR)
                        P.snap(f"armm_{it}_{r_i}", "s_pe")
                        P.do("scalar", lambda s, r_i=r_i: s.activation(A(RPB[r_i % 2], 512),
                                                                       ps[6 + r_i % 2][:, 0:512], AF.Copy),
                             inc="s_act", wait=[f"armm_{it}_{r_i}"] +
                             ([f"arw_{it}_{r_i-2}"] if r_i >= 2 else
                              ([f"arw_{it-1}_{r_i+6}"] if it >= 1 else [])))
                        P.snap(f"arev_{it}_{r_i}", "s_act")
                        P.do("gpsimd", lambda g, it=it, ib=ib, mh=mh, r_i=r_i: [g.dma_start(
                            out=rsin[512 * it + 128 * ib: 512 * it + 128 * (ib + 1), 512 * mh:512 * (mh + 1)],
                            in_=A(RPB[r_i % 2], 512))], inc="s_gp", n=16, wait=[f"arev_{it}_{r_i}"])
                        P.snap(f"arw_{it}_{r_i}", "s_gp")
                        r_i += 1
            P.do("gpsimd", lambda g: g.collective_compute(
                "ReduceScatter", OP.add, replica_groups=[[0, 1, 2, 3], [4, 5, 6, 7]],
                ins=[rsin_t.ap().opt()], outs=[rsout_t.ap().opt()]),
                 inc="s_cc", wait=[("s_gp", P.cnt["s_gp"])])

            barrier()

            for ft in range(NFT):
                t0 = 512 * ft
                fb = ft % 2
                P.do("sync", lambda g, fb=fb, t0=t0: [
                    ld(g, A(XNF[fb] + 512 * mc, 512), xnscr[mc][:, t0:t0 + 512])
                    for mc in range(MC)], inc="s_sy", n=16 * MC,
                     wait=[f"fl1_{ft-2}_done"] if ft >= 2 else [])
                P.snap(f"fxl_{ft}", "s_sy")
                P.do("sync", lambda g, fb=fb, t0=t0: [ld(g, A(WTSL[fb], 512, 0, 8), wtscr[:, t0:t0 + 512])],
                     inc="s_sy", n=16, wait=[f"fwr_{ft-2}"] if ft >= 2 else [])
                P.snap(f"fwtl_{ft}", "s_sy")
                P.do("tensor", lambda te, fb=fb: te.matmul(ps[2 + fb][:, 0:512], A(SELR, 128, 0, 8),
                                                           A(WTSL[fb], 512, 0, 8), start=True, stop=True),
                     inc="s_pe", wait=[f"fwtl_{ft}", "consts"] +
                     ([f"fvp_{ft-2}_7"] if ft >= 2 else []))
                P.snap(f"fwr_{ft}", "s_pe")
                for li, (Wc, src, dst) in enumerate(((W1, None, H1), (W2, H1, H2), (W3, H2, None))):
                    for oc in range(MC):
                        f_i = li * MC + oc
                        for mc in range(MC):
                            wmm = []
                            if src is None:
                                rhs = A(XNF[fb] + 512 * mc, 512)
                                if mc == 0:
                                    wmm += [f"fxl_{ft}", "wffn"]
                            else:
                                rhs = A(src + 512 * mc, 512)
                                wmm.append(f"fev_{ft}_{li-1}_{mc}")
                            if mc == 0:
                                if f_i >= 2:
                                    wmm.append(f"fev_{ft}_{li}_{oc-2}" if oc >= 2 else
                                               f"fev_{ft}_{li-1}_{oc+6}")
                                elif ft >= 1:
                                    wmm.append(f"fev_{ft-1}_2_{oc+6}")
                            P.do("tensor", lambda te, Wc=Wc, oc=oc, mc=mc, rhs=rhs, f_i=f_i:
                                 te.matmul(ps[f_i % 2][:, 0:512], A(Wc + 1024 * mc + 128 * oc, 128), rhs,
                                           start=(mc == 0), stop=(mc == MC - 1)),
                                 inc="s_pe" if mc == MC - 1 else None, wait=wmm)
                        P.snap(f"fmm_{ft}_{li}_{oc}", "s_pe")
                        if li == 0 and oc == MC - 1:
                            P.snap(f"fl1_{ft}_done", "s_pe")
                        if dst is not None:
                            wev = [f"fmm_{ft}_{li}_{oc}"]
                            if ft >= 1:
                                wev.append(f"fmm_{ft-1}_{li+1}_7")
                            P.do("scalar", lambda s, li=li, oc=oc, dst=dst, f_i=f_i:
                                 s.activation(A(dst + 512 * oc, 512), ps[f_i % 2][:, 0:512], AF.Relu,
                                              bias=A(B123 + 8 * li + oc, 1)),
                                 inc="s_act", wait=wev)
                            P.snap(f"fev_{ft}_{li}_{oc}", "s_act")
                        else:
                            yb = oc % 2
                            wy = [f"fmm_{ft}_{li}_{oc}"]
                            if oc >= 2:
                                wy.append(f"fvp_{ft}_{oc-2}")
                            elif ft >= 1:
                                wy.append(f"fvp_{ft-1}_{oc+6}")
                            P.do("scalar", lambda s, yb=yb, f_i=f_i, oc=oc:
                                 s.activation(A(YB[yb], 512), ps[f_i % 2][:, 0:512], AF.Relu,
                                              bias=A(B123 + 16 + oc, 1)),
                                 inc="s_act", wait=wy)
                            P.snap(f"fye_{ft}_{oc}", "s_act")
                            P.snap(f"fev_{ft}_{li}_{oc}", "s_act")
                            P.do("vector", lambda v, yb=yb, fb=fb:
                                 v.tensor_tensor(A(WY, 512), A(YB[yb], 512), ps[2 + fb][:, 0:512], OP.mult),
                                 inc="s_dve", wait=[f"fye_{ft}_{oc}", f"fwr_{ft}"])
                            P.snap(f"fwy_{ft}_{oc}", "s_dve")
                            P.do("vector", lambda v, oc=oc, ft=ft:
                                 v.reduce_sum(A(VPART + 32 * oc + ft, 1), A(WY, 512), axis=AX.X),
                                 inc="s_dve", wait=[f"fwy_{ft}_{oc}"])
                            P.snap(f"fvp_{ft}_{oc}", "s_dve")
            P.do("vector", lambda v: v.reduce_sum(A(VPART + 256, 8),
                                                  A(VPART, 256).rearrange("p (o f) -> p o f", f=32),
                                                  axis=AX.X),
                 inc="s_dve", wait=[f"fvp_{NFT-1}_7"])
            P.snap("fvsum", "s_dve")
            P.do("tensor", lambda te: te.matmul(ps[4][0:8, 0:128], A(VPART + 256, 8), A(IDEN, 128),
                                                start=True, stop=True),
                 inc="s_pe", wait=["fvsum", "consts"])
            P.snap("fvt", "s_pe")
            P.do("scalar", lambda s: s.activation(A(44300, 128, 0, 8), ps[4][0:8, 0:128], AF.Copy),
                 inc="s_act", wait=["fvt"])
            P.snap("fve", "s_act")
            P.do("gpsimd", lambda g: [g.dma_start(out=agvin[:, :], in_=A(44300, 128, 0, 8))],
                 inc="s_gp", n=16, wait=["fve"])
            P.do("gpsimd", lambda g: g.collective_compute(
                "AllGather", OP.bypass, replica_groups=[list(range(8))],
                ins=[agvin_t.ap().opt()], outs=[agvout_t.ap().opt()]),
                 inc="s_cc", wait=[("s_gp", P.cnt["s_gp"])])

            barrier()

            RSA, XRA = 0, 16640
            P.do("sync", lambda g: [
                ld(g, A(VALL + 128 * mc, 128, 0, 8),
                   agvout.rearrange("(e c) p -> e c p", c=8)[:, mc, :])
                for mc in range(MC)], inc="s_sy", n=16 * MC, wait=[("s_cc", 2)])
            P.snap("vl", "s_sy")
            P.do("sync", lambda g: [
                ld(g, ar[:, RSA:RSA + 16640].rearrange("p (g c) -> p g c", c=1040),
                   rsout.rearrange("(g p) c -> p g c", p=128)),
                ld(g, ar[:, XRA:XRA + 16384].rearrange("p (g c) -> p g c", c=1024),
                   xres_in.rearrange("(g p) c -> p g c", p=128)),
            ], inc="s_sy", n=32, wait=[("s_cc", 2)])
            P.snap("rsl", "s_sy")
            for ib in range(16):
                r0 = 128 * ib
                bb = ib % 2
                rb = RSA + 1040 * ib
                xb_ = XRA + 1024 * ib
                P.do("tensor", lambda te, rb=rb: te.matmul(ps[4][0:8, 0:128], A(rb + 1024, 8),
                                                           A(IDEN, 128), start=True, stop=True),
                     inc="s_pe", wait=["rsl", "consts"] + ([f"oie_{ib-1}"] if ib >= 1 else []))
                P.snap(f"oit_{ib}", "s_pe")
                P.do("scalar", lambda s, bb=bb: s.activation(A(INDE[bb], 128, 0, 8), ps[4][0:8, 0:128], AF.Copy),
                     inc="s_act", wait=[f"oit_{ib}"])
                P.snap(f"oie_{ib}", "s_act")
                for mh in range(2):
                    P.do("tensor", lambda te, bb=bb, mh=mh: te.matmul(ps[mh][:, 0:512], A(INDE[bb], 128, 0, 8),
                                                                      A(VALL + 512 * mh, 512, 0, 8),
                                                                      start=True, stop=True),
                         inc="s_pe", wait=[f"oie_{ib}", "vl"] + ([f"oou_{ib-1}"] if ib >= 1 else []))
                    P.snap(f"ofm_{ib}_{mh}", "s_pe")
                P.do("vector", lambda v, bb=bb, rb=rb: [
                    v.tensor_tensor(A(OUTB[bb] + 512 * mh, 512), ps[mh][:, 0:512],
                                    A(rb + 512 * mh, 512), OP.add)
                    for mh in range(2)][-1], inc="s_dve",
                     wait=[f"ofm_{ib}_0", f"ofm_{ib}_1"] +
                     ([f"oow_{ib-2}"] if ib >= 2 else []))
                P.snap(f"oou1_{ib}", "s_dve")
                P.do("vector", lambda v, bb=bb, xb_=xb_: [
                    v.tensor_tensor(A(OUTB[bb] + 512 * mh, 512), A(OUTB[bb] + 512 * mh, 512),
                                    A(xb_ + 512 * mh, 512), OP.add)
                    for mh in range(2)][-1], inc="s_dve",
                     wait=[f"oou1_{ib}"])
                P.snap(f"oou_{ib}", "s_dve")
                P.do("gpsimd", lambda g, bb=bb, r0=r0: [g.dma_start(out=out_t[r0:r0 + 128, :],
                                                                    in_=A(OUTB[bb], 1024))],
                     inc="s_gp", n=16, wait=[f"oou_{ib}"])
                P.snap(f"oow_{ib}", "s_gp")
            P.do("gpsimd", wait=[("s_gp", P.cnt["s_gp"])])

        for engname in ENGS:
            def mk(name):
                def _(eng):
                    program(Walk(name, eng, sems))
                return _
            getattr(block, engname)(mk(engname))

    return nc


_NC_CACHE = []


def kernel(L0_attn_x, L0_attn_wq, L0_attn_wk, L0_attn_wv, L0_attn_wo,
           L0_ffn_gate_w, L0_ffn_gate_b, L0_ffn_w1, L0_ffn_b1,
           L0_ffn_w2, L0_ffn_b2, L0_ffn_w3, L0_ffn_b3):
    x = np.ascontiguousarray(np.asarray(L0_attn_x, dtype=np.float32))
    wq = np.asarray(L0_attn_wq, dtype=np.float32)
    wk = np.asarray(L0_attn_wk, dtype=np.float32)
    wv = np.asarray(L0_attn_wv, dtype=np.float32)
    wo = np.asarray(L0_attn_wo, dtype=np.float32)
    gw = np.ascontiguousarray(np.asarray(L0_ffn_gate_w, dtype=np.float32))
    gb = np.asarray(L0_ffn_gate_b, dtype=np.float32).reshape(1, E)
    w123 = [np.asarray(w, dtype=np.float32) for w in (L0_ffn_w1, L0_ffn_w2, L0_ffn_w3)]
    b123 = [np.asarray(b, dtype=np.float32) for b in (L0_ffn_b1, L0_ffn_b2, L0_ffn_b3)]

    pos = np.arange(T, dtype=np.float32)
    dims = np.arange(64, dtype=np.float32)
    freqs = np.power(np.float32(10000.0), -dims / np.float32(64.0)).astype(np.float32)
    rad = (freqs[:, None] * pos[None, :]).astype(np.float32)
    cosv = np.cos(rad).astype(np.float32)
    sinv = np.sin(rad).astype(np.float32)
    cosT = np.ascontiguousarray(np.concatenate([cosv, cosv], axis=0))
    sgnT = np.ascontiguousarray(np.concatenate([-sinv, sinv], axis=0))
    iden = np.eye(128, dtype=np.float32)
    iden8 = np.eye(8, dtype=np.float32)
    swp = np.zeros((128, 128), dtype=np.float32)
    for d in range(128):
        swp[d, (d + 64) % 128] = 1.0
    pj = np.arange(128)[:, None]
    xx = np.arange(1024)[None, :]
    maskT = np.where(xx < pj + 512, np.float32(-1e33), np.float32(0.0)).astype(np.float32)

    in_maps = []
    for c in range(8):
        b, jg = c // 4, c % 4
        sel = np.zeros((8, 128), dtype=np.float32)
        sel[c, :] = 1.0
        xperm = x if b == 0 else np.ascontiguousarray(x[::-1])
        in_maps.append({
            "x": xperm,
            "xres": np.ascontiguousarray(x[b, 2048 * jg:2048 * (jg + 1), :]),
            "wq": np.ascontiguousarray(wq[2 * jg:2 * jg + 2]),
            "wk": np.ascontiguousarray(wk[2 * jg:2 * jg + 2]),
            "wv": np.ascontiguousarray(wv[2 * jg:2 * jg + 2]),
            "wo": np.ascontiguousarray(wo[2 * jg:2 * jg + 2]),
            "gw": gw, "gb": gb,
            "w1": np.ascontiguousarray(w123[0][c]),
            "w2": np.ascontiguousarray(w123[1][c]),
            "w3": np.ascontiguousarray(w123[2][c]),
            "b1": np.ascontiguousarray(b123[0][c].reshape(8, 128)),
            "b2": np.ascontiguousarray(b123[1][c].reshape(8, 128)),
            "b3": np.ascontiguousarray(b123[2][c].reshape(8, 128)),
            "cosT": cosT, "sgnT": sgnT, "iden": iden, "iden8": iden8,
            "swp64": swp, "maskT": maskT, "selrep": sel,
        })

    if not _NC_CACHE:
        _NC_CACHE.append(build_nc())
    nc = _NC_CACHE[0]
    res = run_bass_kernel_spmd(nc, in_maps, core_ids=list(range(8)))
    out = np.empty((B, T, D), dtype=np.float32)
    for c in range(8):
        b, jg = c // 4, c % 4
        out[b, 2048 * jg:2048 * (jg + 1), :] = res.results[c]["out"]
    return out
